# version 13
# baseline (speedup 1.0000x reference)
import math
import contextlib
import numpy as np
import ml_dtypes
import concourse.bass as bass
import concourse.mybir as mybir
from concourse.bass_utils import run_bass_kernel_spmd

F32 = mybir.dt.float32
BF16 = mybir.dt.bfloat16
U32 = mybir.dt.uint32
AF = mybir.ActivationFunctionType
ALU = mybir.AluOpType
AX = mybir.AxisListType
NEG = -1.0e30

ENGINES = ("tensor", "vector", "scalar", "sync", "gpsimd")


class Ctx:

    def __init__(self, nc):
        self.nc = nc
        self.eng = {e: getattr(nc, e) for e in ENGINES}
        self.esem = {e: nc.alloc_semaphore(f"es_{e}") for e in ENGINES}
        self.count = {e: 0 for e in ENGINES}
        self.known = {e: {f: 0 for f in ENGINES} for e in ENGINES}
        self.hist = {e: [] for e in ENGINES}
        self.writer = {}
        self.readers = {}
        self.dma_sem = {}
        self.dma_cum = {}
        self.sem_pool = []
        self.phase = 0
        self.dram_only = set()
        self.wait_count = 0

    def _need_edge(self, e, src):
        if src[0] == "eng":
            _, f, v = src
            if isinstance(f, tuple) and f[0] == "dmaR":
                key = f[1]
                if self.known[e].get(("dma", key), 0) >= v:
                    return None
                return ("dma", key, v)
            if f == e and f == "tensor":
                return None
            if self.known[e].get(f, 0) >= v:
                return None
            return src
        _, key, cum = src
        if self.known[e].get(("dma", key), 0) >= cum:
            return None
        return src

    def _emit_wait(self, e, src):
        if src[0] == "eng":
            _, f, v = src
            self.eng[e].wait_ge(self.esem[f], v)
            self.wait_count += 1
            snap = self.hist[f][v - 1]
            k = self.known[e]
            for kk, vv in snap.items():
                if k.get(kk, 0) < vv:
                    k[kk] = vv
            if k.get(f, 0) < v:
                k[f] = v
        else:
            _, key, cum = src
            self.eng[e].wait_ge(self.dma_sem[key], cum)
            self.wait_count += 1
            self.known[e][("dma", key)] = cum

    def _pre(self, e, reads, writes):
        edges = []
        for k in reads:
            w = self.writer.get(k)
            if w is not None:
                s = self._need_edge(e, w)
                if s is not None:
                    edges.append(s)
        for k in writes:
            w = self.writer.get(k)
            if w is not None:
                s = self._need_edge(e, w)
                if s is not None:
                    edges.append(s)
            for f, v in self.readers.get(k, {}).items():
                if f == e and e == "tensor":
                    continue
                s = self._need_edge(e, ("eng", f, v))
                if s is not None:
                    edges.append(s)
        best = {}
        for s in edges:
            key = (s[0], s[1])
            if key not in best or best[key][2] < s[2]:
                best[key] = s
        for s in best.values():
            self._emit_wait(e, s)

    def _post_eng(self, e, reads, writes, inc):
        if inc:
            self.count[e] += 1
            snap = dict(self.known[e])
            snap[e] = self.count[e]
            self.hist[e].append(snap)
        v = self.count[e] + (0 if inc else 1)
        for k in writes:
            self.writer[k] = ("eng", e, v)
            self.readers[k] = {}
        for k in reads:
            d = self.readers.setdefault(k, {})
            if d.get(e, 0) < v:
                d[e] = v

    def op(self, e, fname, *args, reads=(), writes=(), war=(), inc=True,
           **kwargs):
        self._pre(e, reads, tuple(writes) + tuple(war))
        ins = getattr(self.eng[e], fname)(*args, **kwargs)
        if inc:
            ins.then_inc(self.esem[e], 1)
        self._post_eng(e, reads, writes, inc)
        return ins

    def raw_pre(self, e, reads=(), writes=()):
        self._pre(e, reads, writes)

    def dma(self, e, out, in_, sem_key, reads=(), writes=(), transpose=False,
            dram_only=False, **kwargs):
        self._pre(e, reads, writes)
        if e == "gpsimd":
            sem_key = ("sw", sem_key)
            if sem_key not in self.dma_sem:
                self.dma_sem[sem_key] = self.nc.alloc_semaphore(
                    f"dsw_{sem_key[1]}")
                self.dma_cum[sem_key] = 0
            if dram_only:
                self.dram_only.add(sem_key)
        else:
            sem_key = (self.phase, sem_key)
            if sem_key not in self.dma_sem:
                if self.sem_pool:
                    h, base = self.sem_pool.pop()
                else:
                    h = self.nc.alloc_semaphore(
                        f"d{len(self.dma_sem)}_{self.phase}")
                    base = 0
                self.dma_sem[sem_key] = h
                self.dma_cum[sem_key] = base
        if transpose:
            ins = self.eng[e].dma_start_transpose(out, in_)
        else:
            ins = self.eng[e].dma_start(out, in_, **kwargs)
        self.dma_cum[sem_key] += 16
        cum = self.dma_cum[sem_key]
        ins.then_inc(self.dma_sem[sem_key], 16)
        for k in reads:
            self.readers.setdefault(k, {})[("dmaR", sem_key)] = cum
        for k in writes:
            self.writer[k] = ("dma", sem_key, cum)
            self.readers[k] = {}
        return ins

    def rebind(self, keys, sem_key):
        k2 = ("sw", sem_key)
        sem_key = k2 if k2 in self.dma_sem else (self.phase, sem_key)
        cum = self.dma_cum[sem_key]
        for k in keys:
            self.writer[k] = ("dma", sem_key, cum)

    def fence(self):
        for key, cum in sorted(self.dma_cum.items(), key=lambda kv: str(kv[0])):
            if key in self.dram_only:
                continue
            if self.known["sync"].get(("dma", key), 0) < cum:
                self._emit_wait("sync", ("dma", key, cum))
        for f in ENGINES:
            if f != "sync" and self.count[f] > 0:
                if self.known["sync"].get(f, 0) < self.count[f]:
                    self._emit_wait("sync", ("eng", f, self.count[f]))
        self.eng["sync"].nop().then_inc(self.esem["sync"], 1)
        self.count["sync"] += 1
        snap = dict(self.known["sync"])
        snap["sync"] = self.count["sync"]
        self.hist["sync"].append(snap)
        for e in ENGINES:
            if e != "sync":
                self._emit_wait(e, ("eng", "sync", self.count["sync"]))
        self.writer = {k: w for k, w in self.writer.items()
                       if w[0] == "dma" and w[1] in self.dram_only}
        self.readers = {}
        keep_s, keep_c = {}, {}
        for k, h in self.dma_sem.items():
            if k[0] == "sw":
                keep_s[k] = h
                keep_c[k] = self.dma_cum[k]
            else:
                self.sem_pool.append((h, self.dma_cum[k]))
        self.dma_sem = keep_s
        self.dma_cum = keep_c
        self.phase += 1

    def finish(self):
        for key, cum in sorted(self.dma_cum.items(), key=lambda kv: str(kv[0])):
            if self.known["sync"].get(("dma", key), 0) < cum:
                self._emit_wait("sync", ("dma", key, cum))


def spans(total, step=512):
    return [(c0, min(step, total - c0)) for c0 in range(0, total, step)]


def make_cfg(tiny=False):
    if tiny:
        c = dict(B=2, T=1024, D=512, H=4, KV=2, DH=128, F=256, E=4, SH=2,
                 QS=128, TW=512, steep_override=2)
    else:
        c = dict(B=2, T=8192, D=2048, H=16, KV=4, DH=128, F=1024, E=16, SH=2,
                 QS=512, TW=1024, steep_override=None)
    c["CPB"] = 4
    c["CH"] = c["T"] // 8
    c["N"] = 2 * c["CH"]
    c["DC"] = c["D"] // 128
    c["FC"] = c["F"] // 128
    c["HPG"] = c["H"] // c["KV"]
    c["QPS"] = c["CH"] // c["QS"]
    c["QT"] = c["QS"] // 128
    c["ZTH"] = 46.0
    c["STH"] = 60.0
    return c


def slopes_of(cfg):
    H = cfg["H"]
    return [2.0 ** (-8.0 * (h + 1) / H) for h in range(H)]


def steep_set(cfg):
    if cfg["steep_override"] is not None:
        return set(range(cfg["steep_override"]))
    sl = slopes_of(cfg)
    return {h for h in range(cfg["H"]) if sl[h] * cfg["QS"] > cfg["STH"]}


def attn_geometry(cfg):
    QS, CH, QPS, KV, HPG = cfg["QS"], cfg["CH"], cfg["QPS"], cfg["KV"], cfg["HPG"]
    sl = slopes_of(cfg)
    ntile = {}
    for s in range(2):
        for j in range(QPS):
            q0s = [(cc if s == 0 else 7 - cc) * CH + j * QS for cc in range(4)]
            cap = (max(q0s) + QS) // 128
            for h in range(cfg["H"]):
                w = cfg["ZTH"] / sl[h]
                n = QS // 128 + int(math.ceil(w / 128.0))
                ntile[(s, j, h)] = min(n, cap)
    pad_lo = 0
    for s in range(2):
        for j in range(QPS):
            q0min = min((cc if s == 0 else 7 - cc) * CH + j * QS
                        for cc in range(4))
            for h in range(cfg["H"]):
                start = q0min + QS - ntile[(s, j, h)] * 128
                pad_lo = max(pad_lo, -start)
    pad_lo = ((pad_lo + 127) // 128) * 128
    win = {}
    for g in range(KV):
        for s in range(2):
            start_rel = 10 ** 9
            end_rel = -(10 ** 9)
            for j in range(QPS):
                for hh in range(HPG):
                    n = ntile[(s, j, g * HPG + hh)]
                    start_rel = min(start_rel, j * QS + QS - n * 128)
                    end_rel = max(end_rel, j * QS + QS)
            win[(g, s)] = (start_rel, end_rel - start_rel)
    return ntile, pad_lo, win


def bias_enum(cfg):
    ntile, _, _ = attn_geometry(cfg)
    idx = {}
    order = []
    for s in range(2):
        for j in range(cfg["QPS"]):
            for h in range(cfg["H"]):
                n = ntile[(s, j, h)]
                for dt in range(cfg["QS"] // 128 - n, cfg["QS"] // 128):
                    idx[(s, j, h, dt)] = len(order)
                    order.append((s, j, h, dt))
    return idx, order


def host_bias_table(cfg, cc):
    sl = slopes_of(cfg)
    stp = steep_set(cfg)
    idx, order = bias_enum(cfg)
    QS, CH, T = cfg["QS"], cfg["CH"], cfg["T"]
    tab = np.zeros((128, len(order)), dtype=np.float64)
    kp = np.arange(128, dtype=np.float64)
    for (s, j, h, dt), col in idx.items():
        q0 = (cc if s == 0 else 7 - cc) * CH + j * QS
        kabs = q0 + dt * 128 + kp
        if h in stp:
            v = np.full(128, sl[h] * dt * 128.0)
        else:
            v = sl[h] * (dt * 128.0 + kp - (QS - 1.0))
        v = np.where(kabs < 0, NEG, v)
        v = np.where(kabs >= T, NEG, v)
        v = np.where(kabs > q0 + QS - 1, NEG, v)
        tab[:, col] = v
    return tab.astype(np.float32)


def host_consts(cfg):
    QS, T, DH = cfg["QS"], cfg["T"], cfg["DH"]
    half = DH // 2
    th = 1.0 / (10000.0 ** (np.arange(half, dtype=np.float64) / half))
    fr = np.outer(np.arange(T, dtype=np.float64), th)
    QT = QS // 128
    md = np.zeros((128, QT * QS), dtype=np.float32)
    kpv = np.arange(128)[:, None]
    qfv = np.arange(QS)[None, :]
    for d in range(QT):
        md[:, d * QS:(d + 1) * QS] = np.where(d * 128 + kpv > qfv, NEG, 0.0)
    stp = sorted(steep_set(cfg))
    sl = slopes_of(cfg)
    ts = np.zeros((128, max(1, len(stp)) * QS), dtype=np.float32)
    for i, h in enumerate(stp):
        ts[:, i * QS:(i + 1) * QS] = (sl[h] * (kpv - qfv)).astype(np.float32)
    return dict(
        ropeK=np.ascontiguousarray(
            np.concatenate([np.cos(fr).T, np.sin(fr).T], 0),
            dtype=np.float32),
        mdiag=md, tsteep=ts,
        ident=np.eye(128, dtype=ml_dtypes.bfloat16),
        identf=np.eye(128, dtype=np.float32),
        ones_col=np.ones((128, 1), dtype=ml_dtypes.bfloat16),
        ones_col_f32=np.ones((128, 1), dtype=np.float32),
        ones_row_f32=np.ones((1, 128), dtype=np.float32),
    )


def host_core_inputs(cfg, core, full):
    T, D, CH, N, DH = cfg["T"], cfg["D"], cfg["CH"], cfg["N"], cfg["DH"]
    b, cc = core // cfg["CPB"], core % cfg["CPB"]
    xb = np.asarray(full["L0_attn_x"])[b]
    rows = list(range(cc * CH, (cc + 1) * CH)) + \
           list(range((7 - cc) * CH, (8 - cc) * CH))
    half = DH // 2
    th = 1.0 / (10000.0 ** (np.arange(half, dtype=np.float64) / half))
    fr = np.outer(np.asarray(rows, dtype=np.float64), th)
    scale = 1.0 / math.sqrt(DH)
    _, pad_lo, win = attn_geometry(cfg)
    cfg_ints = np.zeros((1, 4 * cfg["KV"] + 8), dtype=np.uint32)
    SKC = pad_lo + T
    for g in range(cfg["KV"]):
        for s in range(2):
            start_rel, _ = win[(g, s)]
            base = (cc if s == 0 else 7 - cc) * CH + start_rel
            cfg_ints[0, (g * 2 + s) * 2 + 0] = g * 128 * SKC + pad_lo + base
            cfg_ints[0, (g * 2 + s) * 2 + 1] = \
                (pad_lo + base) * (cfg["KV"] * DH) + g * DH
    E, SH, FC = cfg["E"], cfg["SH"], cfg["FC"]
    f32 = lambda k: np.asarray(full[k], dtype=np.float32)
    D_, DC_, FC_ = cfg["D"], cfg["DC"], cfg["FC"]

    def packw(w):
        Fo = w.shape[1]
        return np.ascontiguousarray(
            w.reshape(DC_, 128, Fo).transpose(1, 0, 2).reshape(128, DC_ * Fo))

    def packe(w):
        ne, _, Fo = w.shape
        t = w.reshape(ne, DC_, 128, FC_, 128).transpose(0, 3, 2, 1, 4)
        return np.ascontiguousarray(t.reshape(ne, FC_, 128, DC_ * 128))

    def packo(w):
        ac = w.shape[0] // 128
        return np.ascontiguousarray(
            w.reshape(ac, 128, w.shape[1]).transpose(1, 0, 2)
            .reshape(128, ac * w.shape[1]))

    im = {
        "x_full": np.ascontiguousarray(xb, dtype=np.float32),
        "x_own": np.ascontiguousarray(xb[rows], dtype=np.float32),
        "wq": packw(f32("L0_attn_Wq")), "wk": packw(f32("L0_attn_Wk")),
        "wv": packw(f32("L0_attn_Wv")), "wo": packo(f32("L0_attn_Wo")),
        "wa": packw(f32("L0_ffn_wa")),
        "rg": packe(f32("L0_ffn_rg")), "rw1": packe(f32("L0_ffn_rw1")),
        "rw2": f32("L0_ffn_rw2"),
        "sg": packe(f32("L0_ffn_sg")), "sw1": packe(f32("L0_ffn_sw1")),
        "sw2": f32("L0_ffn_sw2"),
        "b1g": np.ascontiguousarray(
            np.concatenate([f32("L0_ffn_rgb"), f32("L0_ffn_sgb")], 0)
            .reshape((E + SH) * FC, 128).T),
        "b1w": np.ascontiguousarray(
            np.concatenate([f32("L0_ffn_rb1"), f32("L0_ffn_sb1")], 0)
            .reshape((E + SH) * FC, 128).T),
        "b2": np.ascontiguousarray(
            np.concatenate([f32("L0_ffn_rb2"), f32("L0_ffn_sb2")], 0)),
        "ropeQ": np.ascontiguousarray(
            np.concatenate([(np.cos(fr) * scale).T, (np.sin(fr) * scale).T],
                           0), dtype=np.float32),
        "abias": host_bias_table(cfg, cc),
        "cfg_ints": cfg_ints,
    }
    im.update(host_consts(cfg))
    return im


def build_program(cfg):
    nc = bass.Bass("TRN2", target_bir_lowering=False)
    T, D, N, CH, QS, DH = (cfg["T"], cfg["D"], cfg["N"], cfg["CH"], cfg["QS"],
                           cfg["DH"])
    H, KV, HPG, DC, FC = cfg["H"], cfg["KV"], cfg["HPG"], cfg["DC"], cfg["FC"]
    E, SH, F, TW = cfg["E"], cfg["SH"], cfg["F"], cfg["TW"]
    QPS, QT = cfg["QPS"], cfg["QT"]
    NE = E + SH
    NH = N // 2
    ntile, pad_lo, win = attn_geometry(cfg)
    bidx, border = bias_enum(cfg)
    stp = sorted(steep_set(cfg))
    stp_pos = {h: i for i, h in enumerate(stp)}
    SKC = pad_lo + T
    EPS = 1e-5
    HF = DH // 2

    di = lambda name, shape, dt=F32: nc.dram_tensor(
        name, shape, dt, kind="ExternalInput").ap()
    x_full = di("x_full", [T, D])
    x_own = di("x_own", [N, D])
    AC = H * DH // 128
    wq = di("wq", [128, DC * H * DH]); wk = di("wk", [128, DC * KV * DH])
    wv = di("wv", [128, DC * KV * DH]); wo = di("wo", [128, AC * D])
    wa = di("wa", [128, DC * E])
    rg = di("rg", [E, FC, 128, DC * 128]); rw1 = di("rw1", [E, FC, 128, DC * 128])
    rw2 = di("rw2", [E, F, D])
    sg = di("sg", [SH, FC, 128, DC * 128]); sw1 = di("sw1", [SH, FC, 128, DC * 128])
    sw2 = di("sw2", [SH, F, D])
    b1g = di("b1g", [128, NE * FC]); b1w = di("b1w", [128, NE * FC])
    b2 = di("b2", [NE, D])
    ropeK = di("ropeK", [2 * HF, T])
    ropeQ = di("ropeQ", [2 * HF, N])
    abias = di("abias", [128, len(border)])
    mdiag = di("mdiag", [128, QT * QS])
    tsteep = di("tsteep", [128, max(1, len(stp)) * QS])
    ident = di("ident", [128, 128], BF16)
    identf = di("identf", [128, 128])
    ones_col = di("ones_col", [128, 1], BF16)
    ones_col_f32 = di("ones_col_f32", [128, 1])
    ones_row_f32 = di("ones_row_f32", [1, 128])
    cfg_ints = di("cfg_ints", [1, 4 * KV + 8], U32)
    out = nc.dram_tensor("out", [N, D], F32, kind="ExternalOutput").ap()

    ds = lambda name, shape, dt: nc.dram_tensor(
        name, shape, dt, kind="Internal").ap()
    s_xnf = ds("s_xnf", [T, D], BF16)
    s_xno = ds("s_xno", [N, D], BF16)
    s_k = ds("s_k", [KV * DH, SKC], BF16)
    s_v = ds("s_v", [SKC, KV * DH], BF16)
    s_q = ds("s_q", [H * DH, N], BF16)
    s_attnT = ds("s_attnT", [H * DH, N], BF16)
    s_attn = ds("s_attn", [N, D], F32)
    s_acch = ds("s_acch", [NH, D], F32)
    s_wk = ds("s_wk", [128, DC * KV * DH], BF16)
    s_wv = ds("s_wv", [128, DC * KV * DH], BF16)
    s_wq = ds("s_wq", [128, DC * H * DH], BF16)
    s_wo = ds("s_wo", [128, AC * D], BF16)
    s_wa = ds("s_wa", [128, DC * E], BF16)
    s_wg = ds("s_wg", [NE, FC, 128, DC * 128], BF16)
    s_w1 = ds("s_w1", [NE, FC, 128, DC * 128], BF16)
    s_w2 = ds("s_w2", [NE, F, D], BF16)

    ctx = Ctx(nc)

    def sal(est, name, shape, dt):
        return est.enter_context(nc.sbuf_tensor(name, shape, dt))

    NT = N // 128
    stats = nc.alloc_sbuf_tensor("stats", [128, 2 * NT], F32)
    cfg_sb = nc.alloc_sbuf_tensor("cfg_sb", [1, 4 * KV + 8], U32)
    id_sb = nc.alloc_sbuf_tensor("id_sb", [128, 128], BF16)
    idf_sb = nc.alloc_sbuf_tensor("idf_sb", [128, 128], F32)
    ones_c = nc.alloc_sbuf_tensor("ones_c", [128, 1], BF16)
    ones_cf = nc.alloc_sbuf_tensor("ones_cf", [128, 1], F32)
    ones_r = nc.alloc_sbuf_tensor("ones_r", [1, 128], F32)
    eps_sb = nc.alloc_sbuf_tensor("eps_sb", [128, 1], F32)
    ctx.op("vector", "memset", eps_sb[:], 1e-5, writes=["epsk"])
    ctx.dma("sync", cfg_sb[:], cfg_ints[:], "ld_cfg", writes=["cfg"])
    ctx.dma("sync", id_sb[:], ident[:], "ld_id", writes=["ident"])
    ctx.dma("sync", idf_sb[:], identf[:], "ld_idf", writes=["identf"])
    ctx.dma("sync", ones_c[:], ones_col[:], "ld_oc", writes=["onesc"])
    ctx.dma("sync", ones_cf[:], ones_col_f32[:], "ld_ocf", writes=["onescf"])
    ctx.dma("sync", ones_r[:], ones_row_f32[:], "ld_or", writes=["onesr"])

    pc_keys = []
    ctx.dma("gpsimd", s_wk[:], wk[:], "pck", writes=[("pcwk",)],
            dram_only=True)
    ctx.dma("gpsimd", s_wv[:], wv[:], "pck", writes=[("pcwv",)],
            dram_only=True)
    ctx.rebind([("pcwk",), ("pcwv",)], "pck")
    ctx.dma("gpsimd", s_wq[:], wq[:], "pcq", writes=[("pcwq",)],
            dram_only=True)
    ctx.dma("gpsimd", s_wo[:], wo[:], "pco", writes=[("pcwo",)],
            dram_only=True)
    ctx.dma("gpsimd", s_wa[:], wa[:], "pc", writes=[("pcwa",)],
            dram_only=True)
    pc_keys += [("pcwa",)]
    ctx.rebind(pc_keys, "pc")

    def make_ln(est, x_src, xn_dst, key, tw, store_stats=False):
        xt = [sal(est, f"ln_x{i}_{key}", [128, D], F32) for i in range(4)]
        sq = sal(est, f"ln_sq_{key}", [128, D], F32)
        xnb = [sal(est, f"ln_o{i}_{key}", [128, D], BF16) for i in range(2)]
        sm = [sal(est, f"ln_sm{i}_{key}", [128, 32], F32) for i in range(3)]

        def ln_group(i0, ng):
            gq = (i0 // 4) % 3
            smq = sm[gq]
            K = lambda a: (key, gq, a)
            for j in range(ng):
                i = i0 + j
                p = i % 4
                ctx.dma("sync", xt[p][:], x_src[i * 128:(i + 1) * 128, :],
                        f"{key}x{p}", writes=[(key, "x", p)])
            for j in range(ng):
                p = (i0 + j) % 4
                ctx.op("scalar", "activation", sq[:], xt[p][:], AF.Square,
                       accum_out=smq[:, 4 + j:5 + j],
                       reads=[(key, "x", p)],
                       writes=[(key, "sqd", p, 0), K("s2")])
                ctx.op("vector", "reduce_sum", smq[:, 0 + j:1 + j],
                       xt[p][:], axis=AX.X,
                       reads=[(key, "x", p)], writes=[K("s1")])
            sl = lambda st: smq[:, st * 4:st * 4 + ng]
            ctx.op("vector", "tensor_scalar", sl(2), sl(0), 1.0 / D, None,
                   ALU.mult, reads=[K("s1")], writes=[K("m")])
            ctx.op("vector", "tensor_scalar", sl(3), sl(1), 1.0 / D, None,
                   ALU.mult, reads=[K("s2")], writes=[K("e2")])
            ctx.op("vector", "tensor_tensor", sl(5), sl(2), sl(2), ALU.mult,
                   reads=[K("m")], writes=[K("m2")])
            ctx.op("vector", "tensor_tensor", sl(3), sl(3), sl(5),
                   ALU.subtract, reads=[K("e2"), K("m2")], writes=[K("var")])
            ctx.op("scalar", "activation", sl(4), sl(3), AF.Sqrt,
                   bias=eps_sb[:, 0:1], reads=[K("var"), "epsk"],
                   writes=[K("sd")])
            ctx.op("vector", "reciprocal", sl(6), sl(4),
                   reads=[K("sd")], writes=[K("r")])
            ctx.op("vector", "scalar_tensor_tensor", sl(7), sl(2), -1.0,
                   sl(6), ALU.mult, ALU.mult,
                   reads=[K("m"), K("r")], writes=[K("nmr")])
            for j in range(ng):
                i = i0 + j
                p = i % 4
                po = i % 2
                ctx.op("vector", "tensor_scalar", xnb[po][:], xt[p][:],
                       smq[:, 24 + j:25 + j], smq[:, 28 + j:29 + j],
                       ALU.mult, ALU.add,
                       reads=[(key, "x", p), K("nmr"), K("r")],
                       writes=[(key, "o", po)])
                wi = (i * 128) // tw
                ctx.dma("sync", xn_dst[i * 128:(i + 1) * 128, :], xnb[po][:],
                        f"{key}s{wi}p{po}", reads=[(key, "o", po)],
                        writes=[(key, i)])
                if store_stats:
                    ctx.op("vector", "tensor_scalar", stats[:, i:i + 1],
                           smq[:, 24 + j:25 + j], 1.0, None, ALU.add,
                           reads=[K("r")], writes=[("st1", i)])
                    ctx.op("vector", "tensor_copy",
                           stats[:, NT + i:NT + i + 1],
                           smq[:, 28 + j:29 + j], reads=[K("nmr")],
                           writes=[("st2", i)])
        return ln_group

    with contextlib.ExitStack() as est:
        ln_group = make_ln(est, x_full, s_xnf, "xnf", TW)
        wk_sb = sal(est, "wk_sb", [128, DC * KV * DH], BF16)
        wv_sb = sal(est, "wv_sb", [128, DC * KV * DH], BF16)
        ctx.dma("sync", wk_sb[:], s_wk[:], "ld_wk", reads=[("pcwk",)],
                writes=["wkk"])
        ctx.dma("sync", wv_sb[:], s_wv[:], "ld_wv", reads=[("pcwv",)],
                writes=["wvk"])
        ck = sal(est, "ck", [2 * HF, T], F32)
        ctx.dma("sync", ck[:], ropeK[:], "ld_rk", writes=["rkc", "rks"])
        ckc = ck[0:HF, :]
        cks = ck[HF:2 * HF, :]
        if pad_lo > 0:
            z_sb = sal(est, "z_sb", [128, max(pad_lo, KV * DH)], BF16)
            ctx.op("vector", "memset", z_sb[:], 0.0, writes=["zeros"])
            for gb in range(KV * DH // 128):
                ctx.dma("sync", s_k[gb * 128:(gb + 1) * 128, 0:pad_lo],
                        z_sb[:, 0:pad_lo], "z_k", reads=["zeros"])
            for rb in range(pad_lo // 128):
                ctx.dma("sync", s_v[rb * 128:(rb + 1) * 128, :],
                        z_sb[:, 0:KV * DH], "z_v", reads=["zeros"])
        xs = [[sal(est, f"a1xT{p}_{dc}", [128, TW], BF16)
               for dc in range(DC)] for p in range(2)]
        kro = sal(est, "kro", [128, TW], BF16)
        vev = [sal(est, f"vev{p}", [128, KV * DH], BF16) for p in range(2)]
        tm1 = sal(est, "a1t1", [HF, 512], F32)
        tm2 = sal(est, "a1t2", [HF, 512], F32)
        with nc.psum_tensor("psK", [128, TW], F32) as psK, \
             nc.psum_tensor("psV0", [128, KV * DH], F32) as psV0, \
             nc.psum_tensor("psV1", [128, KV * DH], F32) as psV1:
            psV = [psV0, psV1]
            NW1 = T // TW
            for i0 in range(0, T // 128, 4):
                ln_group(i0, min(4, T // 128 - i0))
            for wi in range(NW1):
                t0 = wi * TW
                p = wi % 2
                for dc in range(DC):
                    ctx.dma("scalar", xs[p][dc][:],
                            s_xnf[t0:t0 + TW, dc * 128:(dc + 1) * 128],
                            f"a1s{p}", transpose=True,
                            reads=[("xnf", (t0 + k * 128) // 128)
                                   for k in range(TW // 128)],
                            writes=[("xT", p, dc)])
                ctx.rebind([("xT", p, dc) for dc in range(DC)], f"a1s{p}")
                for g in range(KV):
                    for dc in range(DC):
                        for c0, w in spans(TW):
                            ctx.op("tensor", "matmul", psK[:, c0:c0 + w],
                                   wk_sb[:, (dc * KV + g) * DH:
                                         (dc * KV + g + 1) * DH],
                                   xs[p][dc][:, c0:c0 + w],
                                   start=(dc == 0), stop=(dc == DC - 1),
                                   inc=(dc == DC - 1),
                                   war=["psK"], reads=["wkk", ("xT", p, dc)],
                                   writes=["psK"] if dc == DC - 1 else [])
                    for c0, w in spans(TW):
                        tsl = slice(t0 + c0, t0 + c0 + w)
                        ctx.op("vector", "tensor_tensor", tm1[:, 0:w],
                               psK[0:HF, c0:c0 + w], ckc[:, tsl], ALU.mult,
                               reads=["psK", "rkc"], writes=["a1t1"])
                        ctx.op("vector", "tensor_tensor", tm2[:, 0:w],
                               psK[HF:DH, c0:c0 + w], cks[:, tsl], ALU.mult,
                               reads=["psK", "rks"], writes=["a1t2"])
                        ctx.op("vector", "tensor_tensor",
                               kro[0:HF, c0:c0 + w], tm1[:, 0:w], tm2[:, 0:w],
                               ALU.subtract, reads=["a1t1", "a1t2"],
                               writes=["kro"])
                        ctx.op("vector", "tensor_tensor", tm1[:, 0:w],
                               psK[0:HF, c0:c0 + w], cks[:, tsl], ALU.mult,
                               reads=["psK", "rks"], writes=["a1t1"])
                        ctx.op("vector", "tensor_tensor", tm2[:, 0:w],
                               psK[HF:DH, c0:c0 + w], ckc[:, tsl], ALU.mult,
                               reads=["psK", "rkc"], writes=["a1t2"])
                        ctx.op("vector", "tensor_tensor",
                               kro[HF:DH, c0:c0 + w], tm1[:, 0:w], tm2[:, 0:w],
                               ALU.add, reads=["a1t1", "a1t2"],
                               writes=["kro"])
                    ctx.dma("scalar", s_k[g * DH:(g + 1) * DH,
                                         pad_lo + t0:pad_lo + t0 + TW],
                            kro[:], f"st_k{g % 2}", reads=["kro"],
                            writes=[("sk", g, wi)])
                for tc in range(TW // 128):
                    pv = tc % 2
                    for dc in range(DC):
                        for c0, w in spans(KV * DH):
                            ctx.op("tensor", "matmul", psV[pv][:, c0:c0 + w],
                                   xs[p][dc][:, tc * 128:(tc + 1) * 128],
                                   wv_sb[:, dc * KV * DH + c0:
                                         dc * KV * DH + c0 + w],
                                   start=(dc == 0), stop=(dc == DC - 1),
                                   inc=(dc == DC - 1),
                                   war=[("psV", pv)], reads=[("xT", p, dc), "wvk"],
                                   writes=[("psV", pv)] if dc == DC - 1 else [])
                    ctx.op("vector", "tensor_copy", vev[pv][:], psV[pv][:],
                           reads=[("psV", pv)], writes=[("vev", pv)])
                    ctx.dma("scalar",
                            s_v[pad_lo + t0 + tc * 128:
                                pad_lo + t0 + (tc + 1) * 128, :],
                            vev[pv][:], f"st_v{pv}", reads=[("vev", pv)],
                            writes=[("sv", wi, tc)])
    ctx.fence()

    with contextlib.ExitStack() as est:
        TW2 = min(TW, N)
        ln_group2 = make_ln(est, x_own, s_xno, "xno", TW2,
                            store_stats=True)
        wq_sb = sal(est, "wq_sb", [128, DC * H * DH], BF16)
        ctx.dma("sync", wq_sb[:], s_wq[:], "ld_wq", reads=[("pcwq",)],
                writes=["wqk"])
        cq = sal(est, "cq", [2 * HF, N], F32)
        ctx.dma("sync", cq[:], ropeQ[:], "ld_rq", writes=["rqc", "rqs"])
        cqc = cq[0:HF, :]
        cqs = cq[HF:2 * HF, :]
        xs2 = [[sal(est, f"a2xT{p}_{dc}", [128, TW2], BF16)
                for dc in range(DC)] for p in range(2)]
        qro = sal(est, "qro", [128, TW2], BF16)
        tq1 = sal(est, "a2t1", [HF, 512], F32)
        tq2 = sal(est, "a2t2", [HF, 512], F32)
        with nc.psum_tensor("psQ", [128, TW2], F32) as psQ:
            NW2 = N // TW2
            for i0 in range(0, N // 128, 4):
                ln_group2(i0, min(4, N // 128 - i0))
            for wi in range(NW2):
                t0 = wi * TW2
                p = wi % 2
                for dc in range(DC):
                    ctx.dma("scalar", xs2[p][dc][:],
                            s_xno[t0:t0 + TW2, dc * 128:(dc + 1) * 128],
                            f"a2s{p}", transpose=True,
                            reads=[("xno", (t0 + k * 128) // 128)
                                   for k in range(TW2 // 128)],
                            writes=[("xT2", p, dc)])
                ctx.rebind([("xT2", p, dc) for dc in range(DC)], f"a2s{p}")
                for h in range(H):
                    for dc in range(DC):
                        for c0, w in spans(TW2):
                            ctx.op("tensor", "matmul", psQ[:, c0:c0 + w],
                                   wq_sb[:, (dc * H + h) * DH:
                                         (dc * H + h + 1) * DH],
                                   xs2[p][dc][:, c0:c0 + w],
                                   start=(dc == 0), stop=(dc == DC - 1),
                                   inc=(dc == DC - 1),
                                   war=["psQ"], reads=["wqk", ("xT2", p, dc)],
                                   writes=["psQ"] if dc == DC - 1 else [])
                    for c0, w in spans(TW2):
                        tsl = slice(t0 + c0, t0 + c0 + w)
                        ctx.op("vector", "tensor_tensor", tq1[:, 0:w],
                               psQ[0:HF, c0:c0 + w], cqc[:, tsl], ALU.mult,
                               reads=["psQ", "rqc"], writes=["a2t1"])
                        ctx.op("vector", "tensor_tensor", tq2[:, 0:w],
                               psQ[HF:DH, c0:c0 + w], cqs[:, tsl], ALU.mult,
                               reads=["psQ", "rqs"], writes=["a2t2"])
                        ctx.op("vector", "tensor_tensor",
                               qro[0:HF, c0:c0 + w], tq1[:, 0:w], tq2[:, 0:w],
                               ALU.subtract, reads=["a2t1", "a2t2"],
                               writes=["qro"])
                        ctx.op("vector", "tensor_tensor", tq1[:, 0:w],
                               psQ[0:HF, c0:c0 + w], cqs[:, tsl], ALU.mult,
                               reads=["psQ", "rqs"], writes=["a2t1"])
                        ctx.op("vector", "tensor_tensor", tq2[:, 0:w],
                               psQ[HF:DH, c0:c0 + w], cqc[:, tsl], ALU.mult,
                               reads=["psQ", "rqc"], writes=["a2t2"])
                        ctx.op("vector", "tensor_tensor",
                               qro[HF:DH, c0:c0 + w], tq1[:, 0:w], tq2[:, 0:w],
                               ALU.add, reads=["a2t1", "a2t2"],
                               writes=["qro"])
                    ctx.dma("scalar", s_q[h * DH:(h + 1) * DH, t0:t0 + TW2],
                            qro[:], f"st_q{h % 2}", reads=["qro"],
                            writes=[("sq", h, wi)])
    ctx.fence()

    with contextlib.ExitStack() as est:
        ab_sb = sal(est, "ab_sb", [128, len(border)], F32)
        md_sb = sal(est, "md_sb", [128, QT * QS], F32)
        ts_sb = sal(est, "ts_sb", [128, max(1, len(stp)) * QS], F32)
        ctx.dma("sync", ab_sb[:], abias[:], "ld_ab", writes=["ab"])
        ctx.dma("sync", md_sb[:], mdiag[:], "ld_md", writes=["md"])
        ctx.dma("sync", ts_sb[:], tsteep[:], "ld_ts", writes=["ts"])
        maxw = max(l for (_, l) in win.values())
        kw = [sal(est, f"kw{p}", [128, maxw], BF16) for p in range(2)]
        vw = [sal(est, f"vw{p}", [128, maxw], BF16) for p in range(2)]
        qw_sb = [sal(est, f"qw{p}", [128, HPG * CH], BF16) for p in range(2)]
        pt = [sal(est, f"pt{p}", [128, QS], BF16) for p in range(4)]
        rc_sb = [sal(est, f"rc_sb{p}", [1, QS], F32) for p in range(2)]
        bcv = [sal(est, f"bcv{p}", [128, QS], F32) for p in range(2)]
        ao_sb = [sal(est, f"ao{p}", [128, QS], BF16) for p in range(2)]
        with nc.psum_tensor("psS0", [128, 512], F32) as psS0, \
             nc.psum_tensor("psS1", [128, 512], F32) as psS1, \
             nc.psum_tensor("psS2", [128, 512], F32) as psS2, \
             nc.psum_tensor("psO0", [128, QS], F32) as psO0, \
             nc.psum_tensor("psO1", [128, QS], F32) as psO1, \
             nc.psum_tensor("psLm", [1, 2 * QS], F32) as psLm, \
             nc.psum_tensor("psB", [128, QS], F32) as psB, \
             nc.sync.register("rko") as rko, \
             nc.sync.register("rvo") as rvo:
            psS = [psS0, psS1, psS2]
            psO = [psO0, psO1]
            psL = [psLm[0:1, 0:QS], psLm[0:1, QS:2 * QS]]
            ei = 0
            for g in range(KV):
                for s in range(2):
                    buf = (g * 2 + s) % 2
                    start_rel, length = win[(g, s)]
                    ctx.raw_pre("sync", ["cfg"])
                    ctx.raw_pre("sync", [], [("kw", buf), ("vw", buf),
                                             ("qw", buf)])
                    ci = (g * 2 + s) * 2
                    nc.sync.reg_load(rko, cfg_sb[0:1, ci:ci + 1])
                    nc.sync.reg_load(rvo, cfg_sb[0:1, ci + 1:ci + 2])
                    ctx.dma("sync", kw[buf][:, 0:length],
                            bass.AP(s_k.tensor, rko, [[SKC, 128], [1, length]]),
                            f"ckw{buf}", writes=[("kw", buf)])
                    ctx.dma("sync", vw[buf][:, 0:length],
                            bass.AP(s_v.tensor, rvo,
                                    [[KV * DH, 128],
                                     [128 * KV * DH, length // 128],
                                     [1, DH]]),
                            f"cvw{buf}", writes=[("vw", buf)])
                    for hh in range(HPG):
                        h = g * HPG + hh
                        ctx.dma("sync", qw_sb[buf][:, hh * CH:(hh + 1) * CH],
                                s_q[h * DH:(h + 1) * DH, s * CH:(s + 1) * CH],
                                f"cqw{buf}", writes=[("qw", buf)])
                    ctx.rebind([("qw", buf)], f"cqw{buf}")
                    tasks = []
                    for j in range(QPS):
                        for hh in range(HPG):
                            h = g * HPG + hh
                            n = ntile[(s, j, h)]
                            for dt in range(QT - n, QT):
                                tasks.append((j, hh, h, n, dt))

                    def stage_qk(i):
                        j, hh, h, n, dt = tasks[i]
                        loc = j * QS + dt * 128 - start_rel
                        sp = i % 3
                        for c0, w in spans(QS):
                            ctx.op("tensor", "matmul",
                                   psS[sp][:, 0:QS][:, c0:c0 + w],
                                   kw[buf][:, loc:loc + 128],
                                   qw_sb[buf][:, hh * CH + j * QS + c0:
                                              hh * CH + j * QS + c0 + w],
                                   start=(c0 == 0), stop=(c0 + w == QS),
                                   inc=(c0 + w == QS),
                                   war=[("psS", sp)],
                                   reads=[("kw", buf), ("qw", buf)],
                                   writes=[("psS", sp)]
                                   if c0 + w == QS else [])
                        if h in stp_pos:
                            i0 = stp_pos[h] * QS
                            ctx.op("vector", "tensor_tensor",
                                   psS[sp][:, 0:QS], psS[sp][:, 0:QS],
                                   ts_sb[:, i0:i0 + QS], ALU.add,
                                   reads=[("psS", sp), "ts"],
                                   writes=[("psS", sp)])
                        if dt >= 0:
                            i0 = dt * QS
                            ctx.op("vector", "tensor_tensor",
                                   psS[sp][:, 0:QS], psS[sp][:, 0:QS],
                                   md_sb[:, i0:i0 + QS], ALU.add,
                                   reads=[("psS", sp), "md"],
                                   writes=[("psS", sp)])

                    def stage_exp(i):
                        j, hh, h, n, dt = tasks[i]
                        sp = i % 3
                        pp = i % 4
                        col = bidx[(s, j, h, dt)]
                        ctx.op("scalar", "activation", pt[pp][:],
                               psS[sp][:, 0:QS], AF.Exp,
                               bias=ab_sb[:, col:col + 1],
                               reads=[("psS", sp), "ab"],
                               writes=[("pt", pp)])

                    def stage_av(i, par):
                        j, hh, h, n, dt = tasks[i]
                        loc = j * QS + dt * 128 - start_rel
                        pp = i % 4
                        first = (dt == QT - n)
                        last = (dt == QT - 1)
                        for c0, w in spans(QS):
                            ctx.op("tensor", "matmul", psO[par][:, c0:c0 + w],
                                   vw[buf][:, loc:loc + 128],
                                   pt[pp][:, c0:c0 + w],
                                   start=first, stop=last,
                                   inc=(last and c0 + w == QS),
                                   war=[("psO", par)],
                                   reads=[("vw", buf), ("pt", pp)],
                                   writes=[("psO", par)]
                                   if (last and c0 + w == QS) else [])
                            ctx.op("tensor", "matmul",
                                   psL[par][:, c0:c0 + w], ones_c[:],
                                   pt[pp][:, c0:c0 + w],
                                   start=first, stop=last,
                                   inc=(last and c0 + w == QS),
                                   war=[("psL", par)],
                                   reads=["onesc", ("pt", pp)],
                                   writes=[("psL", par)]
                                   if (last and c0 + w == QS) else [])
                        return last

                    def stage_evac(i, par):
                        j, hh, h, n, dt = tasks[i]
                        nonlocal ei
                        ctx.op("vector", "reciprocal",
                               rc_sb[par][0:1, :], psL[par],
                               reads=[("psL", par)], writes=[("rc", par)])
                        for c0, w in spans(QS):
                            ctx.op("tensor", "matmul", psB[:, c0:c0 + w],
                                   ones_r[:], rc_sb[par][0:1, c0:c0 + w],
                                   start=True, stop=True, war=["psB"],
                                   reads=[("rc", par), "onesr"],
                                   writes=["psB"])
                        ctx.op("vector", "tensor_copy", bcv[par][:], psB[:],
                               reads=["psB"], writes=[("bcv", par)])
                        ep = ei % 2
                        ei += 1
                        ctx.op("vector", "tensor_tensor", ao_sb[ep][:],
                               psO[par][:], bcv[par][:], ALU.mult,
                               reads=[("psO", par), ("bcv", par)],
                               writes=[("ao", ep)])
                        ctx.dma("sync",
                                s_attnT[h * DH:(h + 1) * DH,
                                        s * CH + j * QS:s * CH + (j + 1) * QS],
                                ao_sb[ep][:], f"cao{ep}",
                                reads=[("ao", ep)],
                                writes=[("saT", h, s, j)])

                    par = ei % 2
                    nt_ = len(tasks)
                    for i in range(nt_ + 2):
                        if i < nt_:
                            stage_qk(i)
                        if 1 <= i < nt_ + 1:
                            stage_exp(i - 1)
                        if i >= 2:
                            was_last = stage_av(i - 2, par)
                            if was_last:
                                stage_evac(i - 2, par)
                                par = ei % 2
    ctx.fence()


    with contextlib.ExitStack() as est:
        wo_sb = sal(est, "wo_sb", [128, AC * D], BF16)
        ctx.dma("sync", wo_sb[:], s_wo[:], "ld_wo", reads=[("pcwo",)],
                writes=["wok"])
        at_in = [sal(est, f"at_in{p}", [128, AC * 128], BF16)
                 for p in range(2)]
        evc = [sal(est, f"c2ev{p}", [128, D], F32) for p in range(2)]
        with nc.psum_tensor("psW0", [128, D], F32) as psW0, \
             nc.psum_tensor("psW1", [128, D], F32) as psW1:
            psW = [psW0, psW1]
            for tc in range(N // 128):
                p = tc % 2
                ctx.dma("sync",
                        at_in[p][:].rearrange("p (a q) -> p a q", a=AC),
                        s_attnT[:, tc * 128:(tc + 1) * 128]
                        .rearrange("(a p) q -> p a q", p=128),
                        f"c2i{p}", writes=[("atin", p)])
                for ac in range(AC):
                    for c0, w in spans(D):
                        ctx.op("tensor", "matmul", psW[p][:, c0:c0 + w],
                               at_in[p][:, ac * 128:(ac + 1) * 128],
                               wo_sb[:, ac * D + c0:ac * D + c0 + w],
                               start=(ac == 0), stop=(ac == AC - 1),
                               inc=(ac == AC - 1),
                               war=[("psW", p)], reads=[("atin", p), "wok"],
                               writes=[("psW", p)] if ac == AC - 1 else [])
                ctx.op("vector", "tensor_copy", evc[p][:], psW[p][:],
                       reads=[("psW", p)], writes=[("c2ev", p)])
                ctx.dma("sync", s_attn[tc * 128:(tc + 1) * 128, :], evc[p][:],
                        f"c2o{p}", reads=[("c2ev", p)],
                        writes=[("sattn", tc)])
    ctx.fence()

    acc = nc.alloc_sbuf_tensor("acc", [128, (NH // 128) * D], F32)
    TCH = NH // 128
    for half in range(2):
        hb = half * NH
        with contextlib.ExitStack() as est:
            xsb = [sal(est, f"bxT{half}_{dc}", [128, NH], BF16)
                   for dc in range(DC)]
            for dc in range(DC):
                ctx.dma("sync", xsb[dc][:],
                        s_xno[hb:hb + NH, dc * 128:(dc + 1) * 128],
                        f"bx{half}", transpose=True, writes=[("bxT", dc)])
            ctx.rebind([("bxT", dc) for dc in range(DC)], f"bx{half}")
            wa_sb = sal(est, f"wa_sb{half}", [128, DC * E], BF16)
            ctx.dma("sync", wa_sb[:], s_wa[:], f"ld_wa{half}",
                    reads=[("pcwa",)], writes=["wak"])
            b1g_sb = sal(est, f"b1g_sb{half}", [128, NE * FC], F32)
            b1w_sb = sal(est, f"b1w_sb{half}", [128, NE * FC], F32)
            b2_sb = sal(est, f"b2_sb{half}", [NE, D], F32)
            ctx.dma("sync", b1g_sb[:], b1g[:], f"lb1g{half}", writes=["b1g"])
            ctx.dma("sync", b1w_sb[:], b1w[:], f"lb1w{half}", writes=["b1w"])
            ctx.dma("sync", b2_sb[:], b2[:], f"lb2{half}", writes=["b2k"])
            comb = sal(est, f"comb{half}", [128, TCH * E], F32)
            combt = sal(est, f"combt{half}", [NE, NH], F32)
            t2a = sal(est, f"t2a{half}", [128, E], F32)
            t2b = sal(est, f"t2b{half}", [128, 8], F32)
            gact = sal(est, f"gact{half}", [128, NH], BF16)
            hsb = [sal(est, f"h{half}_{fc}", [128, NH], BF16)
                   for fc in range(FC)]
            wg_sb = [sal(est, f"wg{half}{p}", [128, DC * 128], BF16)
                     for p in range(4)]
            w1_sb = [sal(est, f"w1{half}{p}", [128, DC * 128], BF16)
                     for p in range(4)]
            w2_sb = [sal(est, f"w2{half}{fc}", [128, D], BF16)
                     for fc in range(FC)]
            with nc.psum_tensor(f"psA{half}", [128, 1024], F32) as psA, \
                 nc.psum_tensor(f"psBB{half}", [128, 1024], F32) as psBB, \
                 nc.psum_tensor(f"psC{half}", [128, 1024], F32) as psC, \
                 nc.psum_tensor(f"psD{half}", [128, 1024], F32) as psD:
                ctx.op("vector", "memset", combt[0:NE, :], 1.0,
                       writes=[("combt", "ones")])
                for tc in range(TCH):
                    for dc in range(DC):
                        ctx.op("tensor", "matmul", psA[:, 0:E],
                               xsb[dc][:, tc * 128:(tc + 1) * 128],
                               wa_sb[:, dc * E:(dc + 1) * E],
                               start=(dc == 0), stop=(dc == DC - 1),
                               inc=(dc == DC - 1),
                               war=["psA"], reads=[("bxT", dc), "wak"],
                               writes=["psA"] if dc == DC - 1 else [])
                    ctx.op("scalar", "activation",
                           comb[:, tc * E:(tc + 1) * E], psA[:, 0:E],
                           AF.Sigmoid, reads=["psA"], writes=[("comb", tc)])
                for tc in range(TCH):
                    csl = comb[:, tc * E:(tc + 1) * E]
                    rk = [("comb", tc)]
                    ctx.op("vector", "reduce_max", t2b[:, 0:1], csl, axis=AX.X,
                           reads=rk, writes=["m1"])
                    ctx.op("vector", "tensor_scalar", t2a[:], csl,
                           t2b[:, 0:1], None, ALU.is_ge,
                           reads=rk + ["m1"], writes=["t2a"])
                    ctx.op("vector", "scalar_tensor_tensor", t2a[:], t2a[:],
                           NEG, csl, ALU.mult, ALU.add,
                           reads=rk + ["t2a"], writes=["t2a"])
                    ctx.op("vector", "reduce_max", t2b[:, 1:2], t2a[:],
                           axis=AX.X, reads=["t2a"], writes=["m2"])
                    ctx.op("vector", "tensor_tensor", t2b[:, 2:3],
                           t2b[:, 0:1], t2b[:, 1:2], ALU.add,
                           reads=["m1", "m2"], writes=["den"])
                    ctx.op("vector", "reciprocal", t2b[:, 3:4], t2b[:, 2:3],
                           reads=["den"], writes=["rden"])
                    ctx.op("vector", "tensor_scalar", t2a[:], csl,
                           t2b[:, 1:2], None, ALU.is_ge,
                           reads=rk + ["m2"], writes=["t2a"])
                    ctx.op("vector", "tensor_tensor", t2a[:], t2a[:], csl,
                           ALU.mult, reads=rk + ["t2a"], writes=["t2a"])
                    ctx.op("vector", "tensor_scalar", csl, t2a[:],
                           t2b[:, 3:4], None, ALU.mult,
                           reads=["t2a", "rden"], writes=[("comb", tc)])
                    ctx.op("tensor", "matmul", psD[0:E, 0:128], csl,
                           idf_sb[:], start=True, stop=True,
                           reads=[("comb", tc), "identf"], writes=["psD"])
                    ctx.op("vector", "tensor_copy",
                           combt[0:E, tc * 128:(tc + 1) * 128],
                           psD[0:E, 0:128], reads=["psD", ("combt", "ones")],
                           writes=[("combt", tc)])
                for tc in range(TCH):
                    for c0, w in spans(D):
                        ctx.op("tensor", "matmul", psA[:, 0:w],
                               combt[0:NE, tc * 128:(tc + 1) * 128],
                               b2_sb[:, c0:c0 + w], start=True, stop=True,
                               reads=[("combt", tc), ("combt", "ones"), "b2k"],
                               writes=["psA"])
                        ctx.op("vector", "tensor_copy",
                               acc[:, tc * D + c0:tc * D + c0 + w],
                               psA[:, 0:w], reads=["psA"],
                               writes=[("acc", tc, c0 // 512)])
                for e in range(NE):
                    routed = e < E
                    wgt = rg[e] if routed else sg[e - E]
                    w1t = rw1[e] if routed else sw1[e - E]
                    w2t = rw2[e] if routed else sw2[e - E]
                    for fc in range(FC):
                        ctx.dma("gpsimd", w2_sb[fc][:],
                                w2t[fc * 128:(fc + 1) * 128, :],
                                f"bw2{fc % 2}", writes=[("w2", fc)])
                    for q in range(2):
                        ctx.rebind([("w2", fc) for fc in range(FC)
                                    if fc % 2 == q], f"bw2{q}")
                    for fc in range(FC):
                        pw = fc % 2
                        pr = fc % 4
                        psg = psA if pw == 0 else psC
                        psw = psBB if pw == 0 else psD
                        kg = "psA" if pw == 0 else "psC"
                        kw1 = "psBB" if pw == 0 else "psD"
                        ctx.dma("gpsimd", wg_sb[pr][:], wgt[fc], f"bwg{pr}",
                                writes=[("wg", pr)])
                        ctx.dma("gpsimd", w1_sb[pr][:], w1t[fc], f"bw1{pr}",
                                writes=[("w1", pr)])
                        for dc in range(DC):
                            for c0, w in spans(NH):
                                ctx.op("tensor", "matmul",
                                       psg[:, 0:NH][:, c0:c0 + w],
                                       wg_sb[pr][:, dc * 128:(dc + 1) * 128],
                                       xsb[dc][:, c0:c0 + w],
                                       start=(dc == 0), stop=(dc == DC - 1),
                                       inc=(dc == DC - 1 and c0 + w == NH),
                                       war=[kg], reads=[("wg", pr), ("bxT", dc)],
                                       writes=[kg] if
                                       (dc == DC - 1 and c0 + w == NH) else [])
                                ctx.op("tensor", "matmul",
                                       psw[:, 0:NH][:, c0:c0 + w],
                                       w1_sb[pr][:, dc * 128:(dc + 1) * 128],
                                       xsb[dc][:, c0:c0 + w],
                                       start=(dc == 0), stop=(dc == DC - 1),
                                       inc=(dc == DC - 1 and c0 + w == NH),
                                       war=[kw1], reads=[("w1", pr), ("bxT", dc)],
                                       writes=[kw1] if
                                       (dc == DC - 1 and c0 + w == NH) else [])
                        bcol = e * FC + fc
                        ctx.op("scalar", "activation", gact[:],
                               psg[:, 0:NH], AF.Gelu,
                               bias=b1g_sb[:, bcol:bcol + 1],
                               reads=[kg, "b1g"], writes=["gact"])
                        ctx.op("vector", "scalar_tensor_tensor", hsb[fc][:],
                               psw[:, 0:NH], b1w_sb[:, bcol:bcol + 1],
                               gact[:], ALU.add, ALU.mult,
                               reads=[kw1, "b1w", "gact"],
                               writes=[("h", fc)])
                    for tc in range(TCH):
                        pw = tc % 2
                        psg = psA if pw == 0 else psC
                        psw = psBB if pw == 0 else psD
                        kg = "psA" if pw == 0 else "psC"
                        kw1 = "psBB" if pw == 0 else "psD"
                        for fc in range(FC):
                            for c0, w in spans(D):
                                tgt, ko = (psg, kg) if c0 < 1024 else (psw, kw1)
                                o0 = c0 % 1024
                                ctx.op("tensor", "matmul",
                                       tgt[:, o0:o0 + w],
                                       hsb[fc][:, tc * 128:(tc + 1) * 128],
                                       w2_sb[fc][:, c0:c0 + w],
                                       start=(fc == 0), stop=(fc == FC - 1),
                                       inc=(fc == FC - 1 and c0 + w == D),
                                       war=[ko], reads=[("h", fc), ("w2", fc)],
                                       writes=[kg, kw1] if
                                       (fc == FC - 1 and c0 + w == D) else [])
                        for part in range(2):
                            c0 = part * 1024
                            if c0 >= D:
                                continue
                            w = min(1024, D - c0)
                            ps = psg if part == 0 else psw
                            ko = kg if part == 0 else kw1
                            ak = ("acc", tc, c0 // 512)
                            ak2 = ("acc", tc, c0 // 512 + 1) if w > 512 else ak
                            if routed:
                                ctx.op("vector", "scalar_tensor_tensor",
                                       acc[:, tc * D + c0:tc * D + c0 + w],
                                       ps[:, 0:w],
                                       comb[:, tc * E + e:tc * E + e + 1],
                                       acc[:, tc * D + c0:tc * D + c0 + w],
                                       ALU.mult, ALU.add,
                                       reads=[ko, ("comb", tc), ak, ak2],
                                       writes=[ak, ak2])
                            else:
                                ctx.op("vector", "tensor_tensor",
                                       acc[:, tc * D + c0:tc * D + c0 + w],
                                       ps[:, 0:w],
                                       acc[:, tc * D + c0:tc * D + c0 + w],
                                       ALU.add, reads=[ko, ak, ak2],
                                       writes=[ak, ak2])
            if half == 0:
                for tc in range(TCH):
                    ctx.dma("sync", s_acch[tc * 128:(tc + 1) * 128, :],
                            acc[:, tc * D:(tc + 1) * D], "st_acc",
                            reads=[("acc", tc, k) for k in range(D // 512)],
                            writes=[("acch", tc)])
        ctx.fence()

    with contextlib.ExitStack() as est:
        NBUF = 4
        xd = [sal(est, f"dx{p}", [128, D], F32) for p in range(NBUF)]
        ad = [sal(est, f"da{p}", [128, D], F32) for p in range(NBUF)]
        fd = [sal(est, f"df{p}", [128, D], F32) for p in range(NBUF)]
        for tc in range(N // 128):
            p = tc % NBUF
            ctx.dma("sync", xd[p][:], x_own[tc * 128:(tc + 1) * 128, :],
                    f"dx{p}", writes=[("dx", p)])
            ctx.dma("gpsimd", ad[p][:], s_attn[tc * 128:(tc + 1) * 128, :],
                    f"da{p}", writes=[("da", p)])
            if tc < TCH:
                ctx.dma("sync", fd[p][:], s_acch[tc * 128:(tc + 1) * 128, :],
                        f"df{p}", writes=[("df", p)])
                facc = fd[p][:]
                frd = [("df", p)]
            else:
                tl = tc - TCH
                facc = acc[:, tl * D:(tl + 1) * D]
                frd = []
            ctx.op("vector", "tensor_scalar", xd[p][:], xd[p][:],
                   stats[:, tc:tc + 1], stats[:, NT + tc:NT + tc + 1],
                   ALU.mult, ALU.add,
                   reads=[("dx", p), ("st1", tc), ("st2", tc)],
                   writes=[("dx", p)])
            ctx.op("vector", "tensor_tensor", xd[p][:], xd[p][:], ad[p][:],
                   ALU.add, reads=[("dx", p), ("da", p)], writes=[("dx", p)])
            ctx.op("vector", "tensor_tensor", xd[p][:], xd[p][:], facc,
                   ALU.add, reads=[("dx", p)] + frd, writes=[("dx", p)])
            ctx.dma("sync", out[tc * 128:(tc + 1) * 128, :], xd[p][:],
                    f"dout{p}", reads=[("dx", p)])
    ctx.finish()
    return nc


def kernel(**inputs):
    cfg = make_cfg(tiny=False)
    nc = build_program(cfg)
    in_maps = [host_core_inputs(cfg, core, inputs) for core in range(8)]
    res = run_bass_kernel_spmd(nc, in_maps, core_ids=list(range(8)))
    B, T, D, CH = cfg["B"], cfg["T"], cfg["D"], cfg["CH"]
    out = np.zeros((B, T, D), dtype=np.float32)
    for core in range(8):
        b, cc = core // 4, core % 4
        o = np.asarray(res.results[core]["out"])
        out[b, cc * CH:(cc + 1) * CH] = o[:CH]
        out[b, (7 - cc) * CH:(8 - cc) * CH] = o[CH:]
    return out


# version 14
# speedup vs baseline: 1.2736x; 1.2736x over previous
import math
import contextlib
import numpy as np
import ml_dtypes
import concourse.bass as bass
import concourse.mybir as mybir
from concourse.bass_utils import run_bass_kernel_spmd

F32 = mybir.dt.float32
BF16 = mybir.dt.bfloat16
U32 = mybir.dt.uint32
AF = mybir.ActivationFunctionType
ALU = mybir.AluOpType
AX = mybir.AxisListType
NEG = -1.0e30

ENGINES = ("tensor", "vector", "scalar", "sync", "gpsimd")


class Ctx:

    def __init__(self, nc):
        self.nc = nc
        self.eng = {e: getattr(nc, e) for e in ENGINES}
        self.esem = {e: nc.alloc_semaphore(f"es_{e}") for e in ENGINES}
        self.count = {e: 0 for e in ENGINES}
        self.known = {e: {f: 0 for f in ENGINES} for e in ENGINES}
        self.hist = {e: [] for e in ENGINES}
        self.writer = {}
        self.readers = {}
        self.dma_sem = {}
        self.dma_cum = {}
        self.sem_pool = []
        self.phase = 0
        self.dram_only = set()
        self.wait_count = 0

    def _need_edge(self, e, src):
        if src[0] == "eng":
            _, f, v = src
            if isinstance(f, tuple) and f[0] == "dmaR":
                key = f[1]
                if self.known[e].get(("dma", key), 0) >= v:
                    return None
                return ("dma", key, v)
            if f == e and f == "tensor":
                return None
            if self.known[e].get(f, 0) >= v:
                return None
            return src
        _, key, cum = src
        if self.known[e].get(("dma", key), 0) >= cum:
            return None
        return src

    def _emit_wait(self, e, src):
        if src[0] == "eng":
            _, f, v = src
            self.eng[e].wait_ge(self.esem[f], v)
            self.wait_count += 1
            snap = self.hist[f][v - 1]
            k = self.known[e]
            for kk, vv in snap.items():
                if k.get(kk, 0) < vv:
                    k[kk] = vv
            if k.get(f, 0) < v:
                k[f] = v
        else:
            _, key, cum = src
            self.eng[e].wait_ge(self.dma_sem[key], cum)
            self.wait_count += 1
            self.known[e][("dma", key)] = cum

    def _pre(self, e, reads, writes):
        edges = []
        for k in reads:
            w = self.writer.get(k)
            if w is not None:
                s = self._need_edge(e, w)
                if s is not None:
                    edges.append(s)
        for k in writes:
            w = self.writer.get(k)
            if w is not None:
                s = self._need_edge(e, w)
                if s is not None:
                    edges.append(s)
            for f, v in self.readers.get(k, {}).items():
                if f == e and e == "tensor":
                    continue
                s = self._need_edge(e, ("eng", f, v))
                if s is not None:
                    edges.append(s)
        best = {}
        for s in edges:
            key = (s[0], s[1])
            if key not in best or best[key][2] < s[2]:
                best[key] = s
        for s in best.values():
            self._emit_wait(e, s)

    def _post_eng(self, e, reads, writes, inc):
        if inc:
            self.count[e] += 1
            snap = dict(self.known[e])
            snap[e] = self.count[e]
            self.hist[e].append(snap)
        v = self.count[e] + (0 if inc else 1)
        for k in writes:
            self.writer[k] = ("eng", e, v)
            self.readers[k] = {}
        for k in reads:
            d = self.readers.setdefault(k, {})
            if d.get(e, 0) < v:
                d[e] = v

    def op(self, e, fname, *args, reads=(), writes=(), war=(), inc=True,
           **kwargs):
        self._pre(e, reads, tuple(writes) + tuple(war))
        ins = getattr(self.eng[e], fname)(*args, **kwargs)
        if inc:
            ins.then_inc(self.esem[e], 1)
        self._post_eng(e, reads, writes, inc)
        return ins

    def raw_pre(self, e, reads=(), writes=()):
        self._pre(e, reads, writes)

    def dma(self, e, out, in_, sem_key, reads=(), writes=(), transpose=False,
            dram_only=False, **kwargs):
        self._pre(e, reads, writes)
        if e == "gpsimd":
            sem_key = ("sw", sem_key)
            if sem_key not in self.dma_sem:
                self.dma_sem[sem_key] = self.nc.alloc_semaphore(
                    f"dsw_{sem_key[1]}")
                self.dma_cum[sem_key] = 0
            if dram_only:
                self.dram_only.add(sem_key)
        else:
            sem_key = (self.phase, sem_key)
            if sem_key not in self.dma_sem:
                if self.sem_pool:
                    h, base = self.sem_pool.pop()
                else:
                    h = self.nc.alloc_semaphore(
                        f"d{len(self.dma_sem)}_{self.phase}")
                    base = 0
                self.dma_sem[sem_key] = h
                self.dma_cum[sem_key] = base
        if transpose:
            ins = self.eng[e].dma_start_transpose(out, in_)
        else:
            ins = self.eng[e].dma_start(out, in_, **kwargs)
        self.dma_cum[sem_key] += 16
        cum = self.dma_cum[sem_key]
        ins.then_inc(self.dma_sem[sem_key], 16)
        for k in reads:
            self.readers.setdefault(k, {})[("dmaR", sem_key)] = cum
        for k in writes:
            self.writer[k] = ("dma", sem_key, cum)
            self.readers[k] = {}
        return ins

    def rebind(self, keys, sem_key):
        k2 = ("sw", sem_key)
        sem_key = k2 if k2 in self.dma_sem else (self.phase, sem_key)
        cum = self.dma_cum[sem_key]
        for k in keys:
            self.writer[k] = ("dma", sem_key, cum)

    def fence(self):
        for key, cum in sorted(self.dma_cum.items(), key=lambda kv: str(kv[0])):
            if key in self.dram_only:
                continue
            if self.known["sync"].get(("dma", key), 0) < cum:
                self._emit_wait("sync", ("dma", key, cum))
        for f in ENGINES:
            if f != "sync" and self.count[f] > 0:
                if self.known["sync"].get(f, 0) < self.count[f]:
                    self._emit_wait("sync", ("eng", f, self.count[f]))
        self.eng["sync"].nop().then_inc(self.esem["sync"], 1)
        self.count["sync"] += 1
        snap = dict(self.known["sync"])
        snap["sync"] = self.count["sync"]
        self.hist["sync"].append(snap)
        for e in ENGINES:
            if e != "sync":
                self._emit_wait(e, ("eng", "sync", self.count["sync"]))
        self.writer = {k: w for k, w in self.writer.items()
                       if w[0] == "dma" and w[1] in self.dram_only}
        self.readers = {}
        keep_s, keep_c = {}, {}
        for k, h in self.dma_sem.items():
            if k[0] == "sw":
                keep_s[k] = h
                keep_c[k] = self.dma_cum[k]
            else:
                self.sem_pool.append((h, self.dma_cum[k]))
        self.dma_sem = keep_s
        self.dma_cum = keep_c
        self.phase += 1

    def finish(self):
        for key, cum in sorted(self.dma_cum.items(), key=lambda kv: str(kv[0])):
            if self.known["sync"].get(("dma", key), 0) < cum:
                self._emit_wait("sync", ("dma", key, cum))


def spans(total, step=512):
    return [(c0, min(step, total - c0)) for c0 in range(0, total, step)]


def make_cfg(tiny=False):
    if tiny:
        c = dict(B=2, T=1024, D=512, H=4, KV=2, DH=128, F=256, E=4, SH=2,
                 QS=128, TW=512, steep_override=2)
    else:
        c = dict(B=2, T=8192, D=2048, H=16, KV=4, DH=128, F=1024, E=16, SH=2,
                 QS=512, TW=1024, steep_override=None)
    c["CPB"] = 4
    c["CH"] = c["T"] // 8
    c["N"] = 2 * c["CH"]
    c["DC"] = c["D"] // 128
    c["FC"] = c["F"] // 128
    c["HPG"] = c["H"] // c["KV"]
    c["QPS"] = c["CH"] // c["QS"]
    c["QT"] = c["QS"] // 128
    c["ZTH"] = 46.0
    c["STH"] = 60.0
    return c


def slopes_of(cfg):
    H = cfg["H"]
    return [2.0 ** (-8.0 * (h + 1) / H) for h in range(H)]


def steep_set(cfg):
    if cfg["steep_override"] is not None:
        return set(range(cfg["steep_override"]))
    sl = slopes_of(cfg)
    return {h for h in range(cfg["H"]) if sl[h] * cfg["QS"] > cfg["STH"]}


def attn_geometry(cfg):
    QS, CH, QPS, KV, HPG = cfg["QS"], cfg["CH"], cfg["QPS"], cfg["KV"], cfg["HPG"]
    sl = slopes_of(cfg)
    ntile = {}
    for s in range(2):
        for j in range(QPS):
            q0s = [(cc if s == 0 else 7 - cc) * CH + j * QS for cc in range(4)]
            cap = (max(q0s) + QS) // 128
            for h in range(cfg["H"]):
                w = cfg["ZTH"] / sl[h]
                n = QS // 128 + int(math.ceil(w / 128.0))
                ntile[(s, j, h)] = min(n, cap)
    pad_lo = 0
    for s in range(2):
        for j in range(QPS):
            q0min = min((cc if s == 0 else 7 - cc) * CH + j * QS
                        for cc in range(4))
            for h in range(cfg["H"]):
                start = q0min + QS - ntile[(s, j, h)] * 128
                pad_lo = max(pad_lo, -start)
    pad_lo = ((pad_lo + 127) // 128) * 128
    win = {}
    for g in range(KV):
        for s in range(2):
            start_rel = 10 ** 9
            end_rel = -(10 ** 9)
            for j in range(QPS):
                for hh in range(HPG):
                    n = ntile[(s, j, g * HPG + hh)]
                    start_rel = min(start_rel, j * QS + QS - n * 128)
                    end_rel = max(end_rel, j * QS + QS)
            win[(g, s)] = (start_rel, end_rel - start_rel)
    return ntile, pad_lo, win


def bias_enum(cfg):
    ntile, _, _ = attn_geometry(cfg)
    idx = {}
    order = []
    for s in range(2):
        for j in range(cfg["QPS"]):
            for h in range(cfg["H"]):
                n = ntile[(s, j, h)]
                for dt in range(cfg["QS"] // 128 - n, cfg["QS"] // 128):
                    idx[(s, j, h, dt)] = len(order)
                    order.append((s, j, h, dt))
    return idx, order


def host_bias_table(cfg, cc):
    sl = slopes_of(cfg)
    stp = steep_set(cfg)
    idx, order = bias_enum(cfg)
    QS, CH, T = cfg["QS"], cfg["CH"], cfg["T"]
    tab = np.zeros((128, len(order)), dtype=np.float64)
    kp = np.arange(128, dtype=np.float64)
    for (s, j, h, dt), col in idx.items():
        q0 = (cc if s == 0 else 7 - cc) * CH + j * QS
        kabs = q0 + dt * 128 + kp
        if h in stp:
            v = np.full(128, sl[h] * dt * 128.0)
        else:
            v = sl[h] * (dt * 128.0 + kp - (QS - 1.0))
        v = np.where(kabs < 0, NEG, v)
        v = np.where(kabs >= T, NEG, v)
        v = np.where(kabs > q0 + QS - 1, NEG, v)
        tab[:, col] = v
    return tab.astype(np.float32)


def host_consts(cfg):
    QS, T, DH = cfg["QS"], cfg["T"], cfg["DH"]
    half = DH // 2
    th = 1.0 / (10000.0 ** (np.arange(half, dtype=np.float64) / half))
    fr = np.outer(np.arange(T, dtype=np.float64), th)
    QT = QS // 128
    md = np.zeros((128, QT * QS), dtype=np.float32)
    kpv = np.arange(128)[:, None]
    qfv = np.arange(QS)[None, :]
    for d in range(QT):
        md[:, d * QS:(d + 1) * QS] = np.where(d * 128 + kpv > qfv, NEG, 0.0)
    stp = sorted(steep_set(cfg))
    sl = slopes_of(cfg)
    ts = np.zeros((128, max(1, len(stp)) * QS), dtype=np.float32)
    for i, h in enumerate(stp):
        ts[:, i * QS:(i + 1) * QS] = (sl[h] * (kpv - qfv)).astype(np.float32)
    return dict(
        ropeK=np.ascontiguousarray(
            np.concatenate([np.cos(fr).T, np.sin(fr).T], 0),
            dtype=np.float32),
        mdiag=md, tsteep=ts,
        ident=np.eye(128, dtype=ml_dtypes.bfloat16),
        identf=np.eye(128, dtype=np.float32),
        ones_col=np.ones((128, 1), dtype=ml_dtypes.bfloat16),
        ones_col_f32=np.ones((128, 1), dtype=np.float32),
        ones_row_f32=np.ones((1, 128), dtype=np.float32),
    )


def host_core_inputs(cfg, core, full):
    T, D, CH, N, DH = cfg["T"], cfg["D"], cfg["CH"], cfg["N"], cfg["DH"]
    b, cc = core // cfg["CPB"], core % cfg["CPB"]
    xb = np.asarray(full["L0_attn_x"])[b]
    rows = list(range(cc * CH, (cc + 1) * CH)) + \
           list(range((7 - cc) * CH, (8 - cc) * CH))
    half = DH // 2
    th = 1.0 / (10000.0 ** (np.arange(half, dtype=np.float64) / half))
    fr = np.outer(np.asarray(rows, dtype=np.float64), th)
    scale = 1.0 / math.sqrt(DH)
    _, pad_lo, win = attn_geometry(cfg)
    cfg_ints = np.zeros((1, 4 * cfg["KV"] + 8), dtype=np.uint32)
    SKC = pad_lo + T
    for g in range(cfg["KV"]):
        for s in range(2):
            start_rel, _ = win[(g, s)]
            base = (cc if s == 0 else 7 - cc) * CH + start_rel
            cfg_ints[0, (g * 2 + s) * 2 + 0] = g * 128 * SKC + pad_lo + base
            cfg_ints[0, (g * 2 + s) * 2 + 1] = \
                (pad_lo + base) * (cfg["KV"] * DH) + g * DH
    E, SH, FC = cfg["E"], cfg["SH"], cfg["FC"]
    f32 = lambda k: np.asarray(full[k], dtype=np.float32)
    D_, DC_, FC_ = cfg["D"], cfg["DC"], cfg["FC"]

    def packw(w):
        Fo = w.shape[1]
        return np.ascontiguousarray(
            w.reshape(DC_, 128, Fo).transpose(1, 0, 2).reshape(128, DC_ * Fo))

    def packe(w):
        ne, _, Fo = w.shape
        t = w.reshape(ne, DC_, 128, FC_, 128).transpose(0, 3, 2, 1, 4)
        return np.ascontiguousarray(t.reshape(ne, FC_, 128, DC_ * 128))

    def packo(w):
        ac = w.shape[0] // 128
        return np.ascontiguousarray(
            w.reshape(ac, 128, w.shape[1]).transpose(1, 0, 2)
            .reshape(128, ac * w.shape[1]))

    im = {
        "x_full": np.ascontiguousarray(xb, dtype=np.float32),
        "x_own": np.ascontiguousarray(xb[rows], dtype=np.float32),
        "wq": packw(f32("L0_attn_Wq")), "wk": packw(f32("L0_attn_Wk")),
        "wv": packw(f32("L0_attn_Wv")), "wo": packo(f32("L0_attn_Wo")),
        "wa": packw(f32("L0_ffn_wa")),
        "rg": packe(f32("L0_ffn_rg")), "rw1": packe(f32("L0_ffn_rw1")),
        "rw2": f32("L0_ffn_rw2"),
        "sg": packe(f32("L0_ffn_sg")), "sw1": packe(f32("L0_ffn_sw1")),
        "sw2": f32("L0_ffn_sw2"),
        "b1g": np.ascontiguousarray(
            np.concatenate([f32("L0_ffn_rgb"), f32("L0_ffn_sgb")], 0)
            .reshape((E + SH) * FC, 128).T),
        "b1w": np.ascontiguousarray(
            np.concatenate([f32("L0_ffn_rb1"), f32("L0_ffn_sb1")], 0)
            .reshape((E + SH) * FC, 128).T),
        "b2": np.ascontiguousarray(
            np.concatenate([f32("L0_ffn_rb2"), f32("L0_ffn_sb2")], 0)),
        "ropeQ": np.ascontiguousarray(
            np.concatenate([(np.cos(fr) * scale).T, (np.sin(fr) * scale).T],
                           0), dtype=np.float32),
        "abias": host_bias_table(cfg, cc),
        "cfg_ints": cfg_ints,
    }
    im.update(host_consts(cfg))
    return im


def build_program(cfg):
    nc = bass.Bass("TRN2", target_bir_lowering=False)
    T, D, N, CH, QS, DH = (cfg["T"], cfg["D"], cfg["N"], cfg["CH"], cfg["QS"],
                           cfg["DH"])
    H, KV, HPG, DC, FC = cfg["H"], cfg["KV"], cfg["HPG"], cfg["DC"], cfg["FC"]
    E, SH, F, TW = cfg["E"], cfg["SH"], cfg["F"], cfg["TW"]
    QPS, QT = cfg["QPS"], cfg["QT"]
    NE = E + SH
    NH = N // 2
    ntile, pad_lo, win = attn_geometry(cfg)
    bidx, border = bias_enum(cfg)
    stp = sorted(steep_set(cfg))
    stp_pos = {h: i for i, h in enumerate(stp)}
    SKC = pad_lo + T
    EPS = 1e-5
    HF = DH // 2

    di = lambda name, shape, dt=F32: nc.dram_tensor(
        name, shape, dt, kind="ExternalInput").ap()
    x_full = di("x_full", [T, D])
    x_own = di("x_own", [N, D])
    AC = H * DH // 128
    wq = di("wq", [128, DC * H * DH]); wk = di("wk", [128, DC * KV * DH])
    wv = di("wv", [128, DC * KV * DH]); wo = di("wo", [128, AC * D])
    wa = di("wa", [128, DC * E])
    rg = di("rg", [E, FC, 128, DC * 128]); rw1 = di("rw1", [E, FC, 128, DC * 128])
    rw2 = di("rw2", [E, F, D])
    sg = di("sg", [SH, FC, 128, DC * 128]); sw1 = di("sw1", [SH, FC, 128, DC * 128])
    sw2 = di("sw2", [SH, F, D])
    b1g = di("b1g", [128, NE * FC]); b1w = di("b1w", [128, NE * FC])
    b2 = di("b2", [NE, D])
    ropeK = di("ropeK", [2 * HF, T])
    ropeQ = di("ropeQ", [2 * HF, N])
    abias = di("abias", [128, len(border)])
    mdiag = di("mdiag", [128, QT * QS])
    tsteep = di("tsteep", [128, max(1, len(stp)) * QS])
    ident = di("ident", [128, 128], BF16)
    identf = di("identf", [128, 128])
    ones_col = di("ones_col", [128, 1], BF16)
    ones_col_f32 = di("ones_col_f32", [128, 1])
    ones_row_f32 = di("ones_row_f32", [1, 128])
    cfg_ints = di("cfg_ints", [1, 4 * KV + 8], U32)
    out = nc.dram_tensor("out", [N, D], F32, kind="ExternalOutput").ap()

    ds = lambda name, shape, dt: nc.dram_tensor(
        name, shape, dt, kind="Internal").ap()
    s_xnf = ds("s_xnf", [T, D], BF16)
    s_xno = ds("s_xno", [N, D], BF16)
    s_k = ds("s_k", [KV * DH, SKC], BF16)
    s_v = ds("s_v", [SKC, KV * DH], BF16)
    s_q = ds("s_q", [H * DH, N], BF16)
    s_attnT = ds("s_attnT", [H * DH, N], BF16)
    s_attn = ds("s_attn", [N, D], F32)
    s_acch = ds("s_acch", [NH, D], F32)
    s_wk = ds("s_wk", [128, DC * KV * DH], BF16)
    s_wv = ds("s_wv", [128, DC * KV * DH], BF16)
    s_wq = ds("s_wq", [128, DC * H * DH], BF16)
    s_wo = ds("s_wo", [128, AC * D], BF16)
    s_wa = ds("s_wa", [128, DC * E], BF16)
    s_wg = ds("s_wg", [NE, FC, 128, DC * 128], BF16)
    s_w1 = ds("s_w1", [NE, FC, 128, DC * 128], BF16)
    s_w2 = ds("s_w2", [NE, F, D], BF16)

    ctx = Ctx(nc)

    def sal(est, name, shape, dt):
        return est.enter_context(nc.sbuf_tensor(name, shape, dt))

    NT = N // 128
    stats = nc.alloc_sbuf_tensor("stats", [128, 2 * NT], F32)
    cfg_sb = nc.alloc_sbuf_tensor("cfg_sb", [1, 4 * KV + 8], U32)
    id_sb = nc.alloc_sbuf_tensor("id_sb", [128, 128], BF16)
    idf_sb = nc.alloc_sbuf_tensor("idf_sb", [128, 128], F32)
    ones_c = nc.alloc_sbuf_tensor("ones_c", [128, 1], BF16)
    ones_cf = nc.alloc_sbuf_tensor("ones_cf", [128, 1], F32)
    ones_r = nc.alloc_sbuf_tensor("ones_r", [1, 128], F32)
    eps_sb = nc.alloc_sbuf_tensor("eps_sb", [128, 1], F32)
    ctx.op("vector", "memset", eps_sb[:], 1e-5, writes=["epsk"])
    ctx.dma("sync", cfg_sb[:], cfg_ints[:], "ld_cfg", writes=["cfg"])
    ctx.dma("sync", id_sb[:], ident[:], "ld_id", writes=["ident"])
    ctx.dma("sync", idf_sb[:], identf[:], "ld_idf", writes=["identf"])
    ctx.dma("sync", ones_c[:], ones_col[:], "ld_oc", writes=["onesc"])
    ctx.dma("sync", ones_cf[:], ones_col_f32[:], "ld_ocf", writes=["onescf"])
    ctx.dma("sync", ones_r[:], ones_row_f32[:], "ld_or", writes=["onesr"])

    pc_keys = []
    ctx.dma("gpsimd", s_wk[:], wk[:], "pck", writes=[("pcwk",)],
            dram_only=True)
    ctx.dma("gpsimd", s_wv[:], wv[:], "pck", writes=[("pcwv",)],
            dram_only=True)
    ctx.rebind([("pcwk",), ("pcwv",)], "pck")
    ctx.dma("gpsimd", s_wq[:], wq[:], "pcq", writes=[("pcwq",)],
            dram_only=True)
    ctx.dma("gpsimd", s_wo[:], wo[:], "pco", writes=[("pcwo",)],
            dram_only=True)
    ctx.dma("gpsimd", s_wa[:], wa[:], "pc", writes=[("pcwa",)],
            dram_only=True)
    pc_keys += [("pcwa",)]
    ctx.rebind(pc_keys, "pc")

    def make_ln(est, x_src, xn_dst, key, tw, store_stats=False):
        xt = [sal(est, f"ln_x{i}_{key}", [128, D], F32) for i in range(4)]
        sq = sal(est, f"ln_sq_{key}", [128, D], F32)
        xnb = [sal(est, f"ln_o{i}_{key}", [128, D], BF16) for i in range(2)]
        sm = [sal(est, f"ln_sm{i}_{key}", [128, 32], F32) for i in range(3)]

        def ln_group(i0, ng):
            gq = (i0 // 4) % 3
            smq = sm[gq]
            K = lambda a: (key, gq, a)
            for j in range(ng):
                i = i0 + j
                p = i % 4
                ctx.dma("sync", xt[p][:], x_src[i * 128:(i + 1) * 128, :],
                        f"{key}x{p}", writes=[(key, "x", p)])
            for j in range(ng):
                p = (i0 + j) % 4
                ctx.op("scalar", "activation", sq[:], xt[p][:], AF.Square,
                       accum_out=smq[:, 4 + j:5 + j],
                       reads=[(key, "x", p)],
                       writes=[(key, "sqd", p, 0), K("s2")])
                ctx.op("vector", "reduce_sum", smq[:, 0 + j:1 + j],
                       xt[p][:], axis=AX.X,
                       reads=[(key, "x", p)], writes=[K("s1")])
            sl = lambda st: smq[:, st * 4:st * 4 + ng]
            ctx.op("vector", "tensor_scalar", sl(2), sl(0), 1.0 / D, None,
                   ALU.mult, reads=[K("s1")], writes=[K("m")])
            ctx.op("vector", "tensor_scalar", sl(3), sl(1), 1.0 / D, None,
                   ALU.mult, reads=[K("s2")], writes=[K("e2")])
            ctx.op("vector", "tensor_tensor", sl(5), sl(2), sl(2), ALU.mult,
                   reads=[K("m")], writes=[K("m2")])
            ctx.op("vector", "tensor_tensor", sl(3), sl(3), sl(5),
                   ALU.subtract, reads=[K("e2"), K("m2")], writes=[K("var")])
            ctx.op("scalar", "activation", sl(4), sl(3), AF.Sqrt,
                   bias=eps_sb[:, 0:1], reads=[K("var"), "epsk"],
                   writes=[K("sd")])
            ctx.op("vector", "reciprocal", sl(6), sl(4),
                   reads=[K("sd")], writes=[K("r")])
            ctx.op("vector", "scalar_tensor_tensor", sl(7), sl(2), -1.0,
                   sl(6), ALU.mult, ALU.mult,
                   reads=[K("m"), K("r")], writes=[K("nmr")])
            for j in range(ng):
                i = i0 + j
                p = i % 4
                po = i % 2
                ctx.op("vector", "tensor_scalar", xnb[po][:], xt[p][:],
                       smq[:, 24 + j:25 + j], smq[:, 28 + j:29 + j],
                       ALU.mult, ALU.add,
                       reads=[(key, "x", p), K("nmr"), K("r")],
                       writes=[(key, "o", po)])
                wi = (i * 128) // tw
                ctx.dma("sync", xn_dst[i * 128:(i + 1) * 128, :], xnb[po][:],
                        f"{key}s{wi}p{po}", reads=[(key, "o", po)],
                        writes=[(key, i)])
                if store_stats:
                    ctx.op("vector", "tensor_scalar", stats[:, i:i + 1],
                           smq[:, 24 + j:25 + j], 1.0, None, ALU.add,
                           reads=[K("r")], writes=[("st1", i)])
                    ctx.op("vector", "tensor_copy",
                           stats[:, NT + i:NT + i + 1],
                           smq[:, 28 + j:29 + j], reads=[K("nmr")],
                           writes=[("st2", i)])
        return ln_group

    with contextlib.ExitStack() as est:
        ln_group = make_ln(est, x_full, s_xnf, "xnf", TW)
        wk_sb = sal(est, "wk_sb", [128, DC * KV * DH], BF16)
        wv_sb = sal(est, "wv_sb", [128, DC * KV * DH], BF16)
        ctx.dma("sync", wk_sb[:], s_wk[:], "ld_wk", reads=[("pcwk",)],
                writes=["wkk"])
        ctx.dma("sync", wv_sb[:], s_wv[:], "ld_wv", reads=[("pcwv",)],
                writes=["wvk"])
        ck = sal(est, "ck", [2 * HF, T], F32)
        ctx.dma("sync", ck[:], ropeK[:], "ld_rk", writes=["rkc", "rks"])
        ckc = ck[0:HF, :]
        cks = ck[HF:2 * HF, :]
        if pad_lo > 0:
            z_sb = sal(est, "z_sb", [128, max(pad_lo, KV * DH)], BF16)
            ctx.op("vector", "memset", z_sb[:], 0.0, writes=["zeros"])
            for gb in range(KV * DH // 128):
                ctx.dma("sync", s_k[gb * 128:(gb + 1) * 128, 0:pad_lo],
                        z_sb[:, 0:pad_lo], "z_k", reads=["zeros"])
            for rb in range(pad_lo // 128):
                ctx.dma("sync", s_v[rb * 128:(rb + 1) * 128, :],
                        z_sb[:, 0:KV * DH], "z_v", reads=["zeros"])
        xs = [[sal(est, f"a1xT{p}_{dc}", [128, TW], BF16)
               for dc in range(DC)] for p in range(2)]
        kro = sal(est, "kro", [128, TW], BF16)
        vev = [sal(est, f"vev{p}", [128, KV * DH], BF16) for p in range(2)]
        tm1 = sal(est, "a1t1", [HF, 512], F32)
        tm2 = sal(est, "a1t2", [HF, 512], F32)
        with nc.psum_tensor("psK", [128, TW], F32) as psK, \
             nc.psum_tensor("psV0", [128, KV * DH], F32) as psV0, \
             nc.psum_tensor("psV1", [128, KV * DH], F32) as psV1:
            psV = [psV0, psV1]
            NW1 = T // TW
            for i0 in range(0, T // 128, 4):
                ln_group(i0, min(4, T // 128 - i0))
            for wi in range(NW1):
                t0 = wi * TW
                p = wi % 2
                for dc in range(DC):
                    ctx.dma("scalar", xs[p][dc][:],
                            s_xnf[t0:t0 + TW, dc * 128:(dc + 1) * 128],
                            f"a1s{p}", transpose=True,
                            reads=[("xnf", (t0 + k * 128) // 128)
                                   for k in range(TW // 128)],
                            writes=[("xT", p, dc)])
                ctx.rebind([("xT", p, dc) for dc in range(DC)], f"a1s{p}")
                for g in range(KV):
                    for dc in range(DC):
                        for c0, w in spans(TW):
                            ctx.op("tensor", "matmul", psK[:, c0:c0 + w],
                                   wk_sb[:, (dc * KV + g) * DH:
                                         (dc * KV + g + 1) * DH],
                                   xs[p][dc][:, c0:c0 + w],
                                   start=(dc == 0), stop=(dc == DC - 1),
                                   inc=(dc == DC - 1),
                                   war=["psK"], reads=["wkk", ("xT", p, dc)],
                                   writes=["psK"] if dc == DC - 1 else [])
                    for c0, w in spans(TW):
                        tsl = slice(t0 + c0, t0 + c0 + w)
                        ctx.op("vector", "tensor_tensor", tm1[:, 0:w],
                               psK[0:HF, c0:c0 + w], ckc[:, tsl], ALU.mult,
                               reads=["psK", "rkc"], writes=["a1t1"])
                        ctx.op("vector", "tensor_tensor", tm2[:, 0:w],
                               psK[HF:DH, c0:c0 + w], cks[:, tsl], ALU.mult,
                               reads=["psK", "rks"], writes=["a1t2"])
                        ctx.op("vector", "tensor_tensor",
                               kro[0:HF, c0:c0 + w], tm1[:, 0:w], tm2[:, 0:w],
                               ALU.subtract, reads=["a1t1", "a1t2"],
                               writes=["kro"])
                        ctx.op("vector", "tensor_tensor", tm1[:, 0:w],
                               psK[0:HF, c0:c0 + w], cks[:, tsl], ALU.mult,
                               reads=["psK", "rks"], writes=["a1t1"])
                        ctx.op("vector", "tensor_tensor", tm2[:, 0:w],
                               psK[HF:DH, c0:c0 + w], ckc[:, tsl], ALU.mult,
                               reads=["psK", "rkc"], writes=["a1t2"])
                        ctx.op("vector", "tensor_tensor",
                               kro[HF:DH, c0:c0 + w], tm1[:, 0:w], tm2[:, 0:w],
                               ALU.add, reads=["a1t1", "a1t2"],
                               writes=["kro"])
                    ctx.dma("scalar", s_k[g * DH:(g + 1) * DH,
                                         pad_lo + t0:pad_lo + t0 + TW],
                            kro[:], f"st_k{g % 2}", reads=["kro"],
                            writes=[("sk", g, wi)])
                for tc in range(TW // 128):
                    pv = tc % 2
                    for dc in range(DC):
                        for c0, w in spans(KV * DH):
                            ctx.op("tensor", "matmul", psV[pv][:, c0:c0 + w],
                                   xs[p][dc][:, tc * 128:(tc + 1) * 128],
                                   wv_sb[:, dc * KV * DH + c0:
                                         dc * KV * DH + c0 + w],
                                   start=(dc == 0), stop=(dc == DC - 1),
                                   inc=(dc == DC - 1),
                                   war=[("psV", pv)], reads=[("xT", p, dc), "wvk"],
                                   writes=[("psV", pv)] if dc == DC - 1 else [])
                    ctx.op("vector", "tensor_copy", vev[pv][:], psV[pv][:],
                           reads=[("psV", pv)], writes=[("vev", pv)])
                    ctx.dma("scalar",
                            s_v[pad_lo + t0 + tc * 128:
                                pad_lo + t0 + (tc + 1) * 128, :],
                            vev[pv][:], f"st_v{pv}", reads=[("vev", pv)],
                            writes=[("sv", wi, tc)])
    ctx.fence()

    with contextlib.ExitStack() as est:
        TW2 = min(TW, N)
        ln_group2 = make_ln(est, x_own, s_xno, "xno", TW2,
                            store_stats=True)
        wq_sb = sal(est, "wq_sb", [128, DC * H * DH], BF16)
        ctx.dma("sync", wq_sb[:], s_wq[:], "ld_wq", reads=[("pcwq",)],
                writes=["wqk"])
        cq = sal(est, "cq", [2 * HF, N], F32)
        ctx.dma("sync", cq[:], ropeQ[:], "ld_rq", writes=["rqc", "rqs"])
        cqc = cq[0:HF, :]
        cqs = cq[HF:2 * HF, :]
        xs2 = [[sal(est, f"a2xT{p}_{dc}", [128, TW2], BF16)
                for dc in range(DC)] for p in range(2)]
        qro = sal(est, "qro", [128, TW2], BF16)
        tq1 = sal(est, "a2t1", [HF, 512], F32)
        tq2 = sal(est, "a2t2", [HF, 512], F32)
        with nc.psum_tensor("psQ", [128, TW2], F32) as psQ:
            NW2 = N // TW2
            for i0 in range(0, N // 128, 4):
                ln_group2(i0, min(4, N // 128 - i0))
            for wi in range(NW2):
                t0 = wi * TW2
                p = wi % 2
                for dc in range(DC):
                    ctx.dma("scalar", xs2[p][dc][:],
                            s_xno[t0:t0 + TW2, dc * 128:(dc + 1) * 128],
                            f"a2s{p}", transpose=True,
                            reads=[("xno", (t0 + k * 128) // 128)
                                   for k in range(TW2 // 128)],
                            writes=[("xT2", p, dc)])
                ctx.rebind([("xT2", p, dc) for dc in range(DC)], f"a2s{p}")
                for h in range(H):
                    for dc in range(DC):
                        for c0, w in spans(TW2):
                            ctx.op("tensor", "matmul", psQ[:, c0:c0 + w],
                                   wq_sb[:, (dc * H + h) * DH:
                                         (dc * H + h + 1) * DH],
                                   xs2[p][dc][:, c0:c0 + w],
                                   start=(dc == 0), stop=(dc == DC - 1),
                                   inc=(dc == DC - 1),
                                   war=["psQ"], reads=["wqk", ("xT2", p, dc)],
                                   writes=["psQ"] if dc == DC - 1 else [])
                    for c0, w in spans(TW2):
                        tsl = slice(t0 + c0, t0 + c0 + w)
                        ctx.op("vector", "tensor_tensor", tq1[:, 0:w],
                               psQ[0:HF, c0:c0 + w], cqc[:, tsl], ALU.mult,
                               reads=["psQ", "rqc"], writes=["a2t1"])
                        ctx.op("vector", "tensor_tensor", tq2[:, 0:w],
                               psQ[HF:DH, c0:c0 + w], cqs[:, tsl], ALU.mult,
                               reads=["psQ", "rqs"], writes=["a2t2"])
                        ctx.op("vector", "tensor_tensor",
                               qro[0:HF, c0:c0 + w], tq1[:, 0:w], tq2[:, 0:w],
                               ALU.subtract, reads=["a2t1", "a2t2"],
                               writes=["qro"])
                        ctx.op("vector", "tensor_tensor", tq1[:, 0:w],
                               psQ[0:HF, c0:c0 + w], cqs[:, tsl], ALU.mult,
                               reads=["psQ", "rqs"], writes=["a2t1"])
                        ctx.op("vector", "tensor_tensor", tq2[:, 0:w],
                               psQ[HF:DH, c0:c0 + w], cqc[:, tsl], ALU.mult,
                               reads=["psQ", "rqc"], writes=["a2t2"])
                        ctx.op("vector", "tensor_tensor",
                               qro[HF:DH, c0:c0 + w], tq1[:, 0:w], tq2[:, 0:w],
                               ALU.add, reads=["a2t1", "a2t2"],
                               writes=["qro"])
                    ctx.dma("scalar", s_q[h * DH:(h + 1) * DH, t0:t0 + TW2],
                            qro[:], f"st_q{h % 2}", reads=["qro"],
                            writes=[("sq", h, wi)])
    ctx.fence()

    with contextlib.ExitStack() as est:
        ab_sb = sal(est, "ab_sb", [128, len(border)], F32)
        md_sb = sal(est, "md_sb", [128, QT * QS], F32)
        ts_sb = sal(est, "ts_sb", [128, max(1, len(stp)) * QS], F32)
        ctx.dma("sync", ab_sb[:], abias[:], "ld_ab", writes=["ab"])
        ctx.dma("sync", md_sb[:], mdiag[:], "ld_md", writes=["md"])
        ctx.dma("sync", ts_sb[:], tsteep[:], "ld_ts", writes=["ts"])
        maxw = max(l for (_, l) in win.values())
        kw = [sal(est, f"kw{p}", [128, maxw], BF16) for p in range(2)]
        vw = [sal(est, f"vw{p}", [128, maxw], BF16) for p in range(2)]
        qw_sb = [sal(est, f"qw{p}", [128, HPG * CH], BF16) for p in range(2)]
        pt = [sal(est, f"pt{p}", [128, QS], BF16) for p in range(4)]
        rc_sb = [sal(est, f"rc_sb{p}", [1, QS], F32) for p in range(2)]
        bcv = [sal(est, f"bcv{p}", [128, QS], F32) for p in range(2)]
        ao_sb = [sal(est, f"ao{p}", [128, QS], BF16) for p in range(2)]
        with nc.psum_tensor("psS0", [128, 512], F32) as psS0, \
             nc.psum_tensor("psS1", [128, 512], F32) as psS1, \
             nc.psum_tensor("psS2", [128, 512], F32) as psS2, \
             nc.psum_tensor("psO0", [128, QS], F32) as psO0, \
             nc.psum_tensor("psO1", [128, QS], F32) as psO1, \
             nc.psum_tensor("psLm", [1, 2 * QS], F32) as psLm, \
             nc.psum_tensor("psB", [128, QS], F32) as psB, \
             nc.sync.register("rko") as rko, \
             nc.sync.register("rvo") as rvo:
            psS = [psS0, psS1, psS2]
            psO = [psO0, psO1]
            psL = [psLm[0:1, 0:QS], psLm[0:1, QS:2 * QS]]
            ei = 0
            for g in range(KV):
                for s in range(2):
                    buf = (g * 2 + s) % 2
                    start_rel, length = win[(g, s)]
                    ctx.raw_pre("sync", ["cfg"])
                    ctx.raw_pre("sync", [], [("kw", buf), ("vw", buf),
                                             ("qw", buf)])
                    ci = (g * 2 + s) * 2
                    nc.sync.reg_load(rko, cfg_sb[0:1, ci:ci + 1])
                    nc.sync.reg_load(rvo, cfg_sb[0:1, ci + 1:ci + 2])
                    ctx.dma("sync", kw[buf][:, 0:length],
                            bass.AP(s_k.tensor, rko, [[SKC, 128], [1, length]]),
                            f"ckw{buf}", writes=[("kw", buf)])
                    ctx.dma("sync", vw[buf][:, 0:length],
                            bass.AP(s_v.tensor, rvo,
                                    [[KV * DH, 128],
                                     [128 * KV * DH, length // 128],
                                     [1, DH]]),
                            f"cvw{buf}", writes=[("vw", buf)])
                    for hh in range(HPG):
                        h = g * HPG + hh
                        ctx.dma("sync", qw_sb[buf][:, hh * CH:(hh + 1) * CH],
                                s_q[h * DH:(h + 1) * DH, s * CH:(s + 1) * CH],
                                f"cqw{buf}", writes=[("qw", buf)])
                    ctx.rebind([("qw", buf)], f"cqw{buf}")
                    tasks = []
                    for j in range(QPS):
                        for hh in range(HPG):
                            h = g * HPG + hh
                            n = ntile[(s, j, h)]
                            for dt in range(QT - n, QT):
                                tasks.append((j, hh, h, n, dt))

                    def stage_qk(i):
                        j, hh, h, n, dt = tasks[i]
                        loc = j * QS + dt * 128 - start_rel
                        sp = i % 3
                        for c0, w in spans(QS):
                            ctx.op("tensor", "matmul",
                                   psS[sp][:, 0:QS][:, c0:c0 + w],
                                   kw[buf][:, loc:loc + 128],
                                   qw_sb[buf][:, hh * CH + j * QS + c0:
                                              hh * CH + j * QS + c0 + w],
                                   start=(c0 == 0), stop=(c0 + w == QS),
                                   inc=(c0 + w == QS),
                                   war=[("psS", sp)],
                                   reads=[("kw", buf), ("qw", buf)],
                                   writes=[("psS", sp)]
                                   if c0 + w == QS else [])
                        if h in stp_pos:
                            i0 = stp_pos[h] * QS
                            ctx.op("vector", "tensor_tensor",
                                   psS[sp][:, 0:QS], psS[sp][:, 0:QS],
                                   ts_sb[:, i0:i0 + QS], ALU.add,
                                   reads=[("psS", sp), "ts"],
                                   writes=[("psS", sp)])
                        if dt >= 0:
                            i0 = dt * QS
                            ctx.op("vector", "tensor_tensor",
                                   psS[sp][:, 0:QS], psS[sp][:, 0:QS],
                                   md_sb[:, i0:i0 + QS], ALU.add,
                                   reads=[("psS", sp), "md"],
                                   writes=[("psS", sp)])

                    def stage_exp(i):
                        j, hh, h, n, dt = tasks[i]
                        sp = i % 3
                        pp = i % 4
                        col = bidx[(s, j, h, dt)]
                        ctx.op("scalar", "activation", pt[pp][:],
                               psS[sp][:, 0:QS], AF.Exp,
                               bias=ab_sb[:, col:col + 1],
                               reads=[("psS", sp), "ab"],
                               writes=[("pt", pp)])

                    def stage_av(i, par):
                        j, hh, h, n, dt = tasks[i]
                        loc = j * QS + dt * 128 - start_rel
                        pp = i % 4
                        first = (dt == QT - n)
                        last = (dt == QT - 1)
                        for c0, w in spans(QS):
                            ctx.op("tensor", "matmul", psO[par][:, c0:c0 + w],
                                   vw[buf][:, loc:loc + 128],
                                   pt[pp][:, c0:c0 + w],
                                   start=first, stop=last,
                                   inc=(last and c0 + w == QS),
                                   war=[("psO", par)],
                                   reads=[("vw", buf), ("pt", pp)],
                                   writes=[("psO", par)]
                                   if (last and c0 + w == QS) else [])
                            ctx.op("tensor", "matmul",
                                   psL[par][:, c0:c0 + w], ones_c[:],
                                   pt[pp][:, c0:c0 + w],
                                   start=first, stop=last,
                                   inc=(last and c0 + w == QS),
                                   war=[("psL", par)],
                                   reads=["onesc", ("pt", pp)],
                                   writes=[("psL", par)]
                                   if (last and c0 + w == QS) else [])
                        return last

                    def stage_evac(i, par):
                        j, hh, h, n, dt = tasks[i]
                        nonlocal ei
                        ctx.op("vector", "reciprocal",
                               rc_sb[par][0:1, :], psL[par],
                               reads=[("psL", par)], writes=[("rc", par)])
                        for c0, w in spans(QS):
                            ctx.op("tensor", "matmul", psB[:, c0:c0 + w],
                                   ones_r[:], rc_sb[par][0:1, c0:c0 + w],
                                   start=True, stop=True, war=["psB"],
                                   reads=[("rc", par), "onesr"],
                                   writes=["psB"])
                        ctx.op("vector", "tensor_copy", bcv[par][:], psB[:],
                               reads=["psB"], writes=[("bcv", par)])
                        ep = ei % 2
                        ei += 1
                        ctx.op("vector", "tensor_tensor", ao_sb[ep][:],
                               psO[par][:], bcv[par][:], ALU.mult,
                               reads=[("psO", par), ("bcv", par)],
                               writes=[("ao", ep)])
                        ctx.dma("sync",
                                s_attnT[h * DH:(h + 1) * DH,
                                        s * CH + j * QS:s * CH + (j + 1) * QS],
                                ao_sb[ep][:], f"cao{ep}",
                                reads=[("ao", ep)],
                                writes=[("saT", h, s, j)])

                    par = ei % 2
                    nt_ = len(tasks)
                    for i in range(nt_ + 2):
                        if i < nt_:
                            stage_qk(i)
                        if 1 <= i < nt_ + 1:
                            stage_exp(i - 1)
                        if i >= 2:
                            was_last = stage_av(i - 2, par)
                            if was_last:
                                stage_evac(i - 2, par)
                                par = ei % 2
    ctx.fence()


    with contextlib.ExitStack() as est:
        wo_sb = sal(est, "wo_sb", [128, AC * D], BF16)
        ctx.dma("sync", wo_sb[:], s_wo[:], "ld_wo", reads=[("pcwo",)],
                writes=["wok"])
        at_in = [sal(est, f"at_in{p}", [128, AC * 128], BF16)
                 for p in range(2)]
        evc = [sal(est, f"c2ev{p}", [128, D], F32) for p in range(2)]
        with nc.psum_tensor("psW0", [128, D], F32) as psW0, \
             nc.psum_tensor("psW1", [128, D], F32) as psW1:
            psW = [psW0, psW1]
            for tc in range(N // 128):
                p = tc % 2
                ctx.dma("sync",
                        at_in[p][:].rearrange("p (a q) -> p a q", a=AC),
                        s_attnT[:, tc * 128:(tc + 1) * 128]
                        .rearrange("(a p) q -> p a q", p=128),
                        f"c2i{p}", writes=[("atin", p)])
                for ac in range(AC):
                    for c0, w in spans(D):
                        ctx.op("tensor", "matmul", psW[p][:, c0:c0 + w],
                               at_in[p][:, ac * 128:(ac + 1) * 128],
                               wo_sb[:, ac * D + c0:ac * D + c0 + w],
                               start=(ac == 0), stop=(ac == AC - 1),
                               inc=(ac == AC - 1),
                               war=[("psW", p)], reads=[("atin", p), "wok"],
                               writes=[("psW", p)] if ac == AC - 1 else [])
                ctx.op("vector", "tensor_copy", evc[p][:], psW[p][:],
                       reads=[("psW", p)], writes=[("c2ev", p)])
                ctx.dma("sync", s_attn[tc * 128:(tc + 1) * 128, :], evc[p][:],
                        f"c2o{p}", reads=[("c2ev", p)],
                        writes=[("sattn", tc)])
    ctx.fence()

    acc = nc.alloc_sbuf_tensor("acc", [128, (NH // 128) * D], F32)
    TCH = NH // 128
    for half in range(2):
        hb = half * NH
        with contextlib.ExitStack() as est:
            xsb = [sal(est, f"bxT{half}_{dc}", [128, NH], BF16)
                   for dc in range(DC)]
            for dc in range(DC):
                ctx.dma("sync", xsb[dc][:],
                        s_xno[hb:hb + NH, dc * 128:(dc + 1) * 128],
                        f"bx{half}", transpose=True, writes=[("bxT", dc)])
            ctx.rebind([("bxT", dc) for dc in range(DC)], f"bx{half}")
            wa_sb = sal(est, f"wa_sb{half}", [128, DC * E], BF16)
            ctx.dma("sync", wa_sb[:], s_wa[:], f"ld_wa{half}",
                    reads=[("pcwa",)], writes=["wak"])
            b1g_sb = sal(est, f"b1g_sb{half}", [128, NE * FC], F32)
            b1w_sb = sal(est, f"b1w_sb{half}", [128, NE * FC], F32)
            b2_sb = sal(est, f"b2_sb{half}", [NE, D], F32)
            ctx.dma("sync", b1g_sb[:], b1g[:], f"lb1g{half}", writes=["b1g"])
            ctx.dma("sync", b1w_sb[:], b1w[:], f"lb1w{half}", writes=["b1w"])
            ctx.dma("sync", b2_sb[:], b2[:], f"lb2{half}", writes=["b2k"])
            comb = sal(est, f"comb{half}", [128, TCH * E], F32)
            combt = sal(est, f"combt{half}", [NE, NH], F32)
            t2a = sal(est, f"t2a{half}", [128, E], F32)
            t2b = sal(est, f"t2b{half}", [128, 8], F32)
            gact = sal(est, f"gact{half}", [128, NH], BF16)
            hsb = [sal(est, f"h{half}_{fc}", [128, NH], BF16)
                   for fc in range(FC)]
            wg_sb = [sal(est, f"wg{half}{p}", [128, DC * 128], BF16)
                     for p in range(4)]
            w1_sb = [sal(est, f"w1{half}{p}", [128, DC * 128], BF16)
                     for p in range(4)]
            w2_sb = [sal(est, f"w2{half}{fc}", [128, D], BF16)
                     for fc in range(FC)]
            with nc.psum_tensor(f"psA{half}", [128, 1024], F32) as psA, \
                 nc.psum_tensor(f"psBB{half}", [128, 1024], F32) as psBB, \
                 nc.psum_tensor(f"psC{half}", [128, 1024], F32) as psC, \
                 nc.psum_tensor(f"psD{half}", [128, 1024], F32) as psD:
                ctx.op("vector", "memset", combt[0:NE, :], 1.0,
                       writes=[("combt", "ones")])
                for tc in range(TCH):
                    for dc in range(DC):
                        ctx.op("tensor", "matmul", psA[:, 0:E],
                               xsb[dc][:, tc * 128:(tc + 1) * 128],
                               wa_sb[:, dc * E:(dc + 1) * E],
                               start=(dc == 0), stop=(dc == DC - 1),
                               inc=(dc == DC - 1),
                               war=["psA"], reads=[("bxT", dc), "wak"],
                               writes=["psA"] if dc == DC - 1 else [])
                    ctx.op("scalar", "activation",
                           comb[:, tc * E:(tc + 1) * E], psA[:, 0:E],
                           AF.Sigmoid, reads=["psA"], writes=[("comb", tc)])
                for tc in range(TCH):
                    csl = comb[:, tc * E:(tc + 1) * E]
                    rk = [("comb", tc)]
                    ctx.op("vector", "reduce_max", t2b[:, 0:1], csl, axis=AX.X,
                           reads=rk, writes=["m1"])
                    ctx.op("vector", "tensor_scalar", t2a[:], csl,
                           t2b[:, 0:1], None, ALU.is_ge,
                           reads=rk + ["m1"], writes=["t2a"])
                    ctx.op("vector", "scalar_tensor_tensor", t2a[:], t2a[:],
                           NEG, csl, ALU.mult, ALU.add,
                           reads=rk + ["t2a"], writes=["t2a"])
                    ctx.op("vector", "reduce_max", t2b[:, 1:2], t2a[:],
                           axis=AX.X, reads=["t2a"], writes=["m2"])
                    ctx.op("vector", "tensor_tensor", t2b[:, 2:3],
                           t2b[:, 0:1], t2b[:, 1:2], ALU.add,
                           reads=["m1", "m2"], writes=["den"])
                    ctx.op("vector", "reciprocal", t2b[:, 3:4], t2b[:, 2:3],
                           reads=["den"], writes=["rden"])
                    ctx.op("vector", "tensor_scalar", t2a[:], csl,
                           t2b[:, 1:2], None, ALU.is_ge,
                           reads=rk + ["m2"], writes=["t2a"])
                    ctx.op("vector", "tensor_tensor", t2a[:], t2a[:], csl,
                           ALU.mult, reads=rk + ["t2a"], writes=["t2a"])
                    ctx.op("vector", "tensor_scalar", csl, t2a[:],
                           t2b[:, 3:4], None, ALU.mult,
                           reads=["t2a", "rden"], writes=[("comb", tc)])
                    ctx.op("tensor", "matmul", psD[0:E, 0:128], csl,
                           idf_sb[:], start=True, stop=True,
                           reads=[("comb", tc), "identf"], writes=["psD"])
                    ctx.op("vector", "tensor_copy",
                           combt[0:E, tc * 128:(tc + 1) * 128],
                           psD[0:E, 0:128], reads=["psD", ("combt", "ones")],
                           writes=[("combt", tc)])
                for tc in range(TCH):
                    for c0, w in spans(D):
                        ctx.op("tensor", "matmul", psA[:, 0:w],
                               combt[0:NE, tc * 128:(tc + 1) * 128],
                               b2_sb[:, c0:c0 + w], start=True, stop=True,
                               reads=[("combt", tc), ("combt", "ones"), "b2k"],
                               writes=["psA"])
                        ctx.op("vector", "tensor_copy",
                               acc[:, tc * D + c0:tc * D + c0 + w],
                               psA[:, 0:w], reads=["psA"],
                               writes=[("acc", tc, c0 // 512)])
                def wslab(ei_, kind):
                    routed_ = ei_ < E
                    if kind == "g":
                        return rg[ei_] if routed_ else sg[ei_ - E]
                    if kind == "1":
                        return rw1[ei_] if routed_ else sw1[ei_ - E]
                    return rw2[ei_] if routed_ else sw2[ei_ - E]

                def load_l1(ei_, fc):
                    pr = fc % 4
                    ctx.dma("gpsimd", wg_sb[pr][:], wslab(ei_, "g")[fc],
                            f"bwg{pr}", writes=[("wg", pr)])
                    ctx.dma("gpsimd", w1_sb[pr][:], wslab(ei_, "1")[fc],
                            f"bw1{pr}", writes=[("w1", pr)])
                load_l1(0, 0)
                load_l1(0, 1)
                for e in range(NE):
                    routed = e < E
                    w2t = wslab(e, "2")
                    for fc in range(FC):
                        ctx.dma("gpsimd", w2_sb[fc][:],
                                w2t[fc * 128:(fc + 1) * 128, :],
                                f"bw2{fc % 2}", writes=[("w2", fc)])
                    for q in range(2):
                        ctx.rebind([("w2", fc) for fc in range(FC)
                                    if fc % 2 == q], f"bw2{q}")
                    for fc in range(FC):
                        pw = fc % 2
                        pr = fc % 4
                        psg = psA if pw == 0 else psC
                        psw = psBB if pw == 0 else psD
                        kg = "psA" if pw == 0 else "psC"
                        kw1 = "psBB" if pw == 0 else "psD"
                        for dc in range(DC):
                            for c0, w in spans(NH):
                                ctx.op("tensor", "matmul",
                                       psg[:, 0:NH][:, c0:c0 + w],
                                       wg_sb[pr][:, dc * 128:(dc + 1) * 128],
                                       xsb[dc][:, c0:c0 + w],
                                       start=(dc == 0), stop=(dc == DC - 1),
                                       inc=(dc == DC - 1 and c0 + w == NH),
                                       war=[kg], reads=[("wg", pr), ("bxT", dc)],
                                       writes=[kg] if
                                       (dc == DC - 1 and c0 + w == NH) else [])
                                ctx.op("tensor", "matmul",
                                       psw[:, 0:NH][:, c0:c0 + w],
                                       w1_sb[pr][:, dc * 128:(dc + 1) * 128],
                                       xsb[dc][:, c0:c0 + w],
                                       start=(dc == 0), stop=(dc == DC - 1),
                                       inc=(dc == DC - 1 and c0 + w == NH),
                                       war=[kw1], reads=[("w1", pr), ("bxT", dc)],
                                       writes=[kw1] if
                                       (dc == DC - 1 and c0 + w == NH) else [])
                        bcol = e * FC + fc
                        ctx.op("scalar", "activation", gact[:],
                               psg[:, 0:NH], AF.Gelu,
                               bias=b1g_sb[:, bcol:bcol + 1],
                               reads=[kg, "b1g"], writes=["gact"])
                        ctx.op("vector", "scalar_tensor_tensor", hsb[fc][:],
                               psw[:, 0:NH], b1w_sb[:, bcol:bcol + 1],
                               gact[:], ALU.add, ALU.mult,
                               reads=[kw1, "b1w", "gact"],
                               writes=[("h", fc)])
                        nfc = fc + 2
                        if nfc < FC:
                            load_l1(e, nfc)
                        elif e + 1 < NE:
                            load_l1(e + 1, nfc - FC)
                    for tc in range(TCH):
                        pw = tc % 2
                        psg = psA if pw == 0 else psC
                        psw = psBB if pw == 0 else psD
                        kg = "psA" if pw == 0 else "psC"
                        kw1 = "psBB" if pw == 0 else "psD"
                        for fc in range(FC):
                            for c0, w in spans(D):
                                tgt, ko = (psg, kg) if c0 < 1024 else (psw, kw1)
                                o0 = c0 % 1024
                                ctx.op("tensor", "matmul",
                                       tgt[:, o0:o0 + w],
                                       hsb[fc][:, tc * 128:(tc + 1) * 128],
                                       w2_sb[fc][:, c0:c0 + w],
                                       start=(fc == 0), stop=(fc == FC - 1),
                                       inc=(fc == FC - 1 and c0 + w == D),
                                       war=[ko], reads=[("h", fc), ("w2", fc)],
                                       writes=[kg, kw1] if
                                       (fc == FC - 1 and c0 + w == D) else [])
                        for part in range(2):
                            c0 = part * 1024
                            if c0 >= D:
                                continue
                            w = min(1024, D - c0)
                            ps = psg if part == 0 else psw
                            ko = kg if part == 0 else kw1
                            ak = ("acc", tc, c0 // 512)
                            ak2 = ("acc", tc, c0 // 512 + 1) if w > 512 else ak
                            if routed:
                                ctx.op("vector", "scalar_tensor_tensor",
                                       acc[:, tc * D + c0:tc * D + c0 + w],
                                       ps[:, 0:w],
                                       comb[:, tc * E + e:tc * E + e + 1],
                                       acc[:, tc * D + c0:tc * D + c0 + w],
                                       ALU.mult, ALU.add,
                                       reads=[ko, ("comb", tc), ak, ak2],
                                       writes=[ak, ak2])
                            else:
                                ctx.op("vector", "tensor_tensor",
                                       acc[:, tc * D + c0:tc * D + c0 + w],
                                       ps[:, 0:w],
                                       acc[:, tc * D + c0:tc * D + c0 + w],
                                       ALU.add, reads=[ko, ak, ak2],
                                       writes=[ak, ak2])
            if half == 0:
                for tc in range(TCH):
                    ctx.dma("sync", s_acch[tc * 128:(tc + 1) * 128, :],
                            acc[:, tc * D:(tc + 1) * D], "st_acc",
                            reads=[("acc", tc, k) for k in range(D // 512)],
                            writes=[("acch", tc)])
        ctx.fence()

    with contextlib.ExitStack() as est:
        NBUF = 4
        xd = [sal(est, f"dx{p}", [128, D], F32) for p in range(NBUF)]
        ad = [sal(est, f"da{p}", [128, D], F32) for p in range(NBUF)]
        fd = [sal(est, f"df{p}", [128, D], F32) for p in range(NBUF)]
        for tc in range(N // 128):
            p = tc % NBUF
            ctx.dma("sync", xd[p][:], x_own[tc * 128:(tc + 1) * 128, :],
                    f"dx{p}", writes=[("dx", p)])
            ctx.dma("gpsimd", ad[p][:], s_attn[tc * 128:(tc + 1) * 128, :],
                    f"da{p}", writes=[("da", p)])
            if tc < TCH:
                ctx.dma("sync", fd[p][:], s_acch[tc * 128:(tc + 1) * 128, :],
                        f"df{p}", writes=[("df", p)])
                facc = fd[p][:]
                frd = [("df", p)]
            else:
                tl = tc - TCH
                facc = acc[:, tl * D:(tl + 1) * D]
                frd = []
            ctx.op("vector", "tensor_scalar", xd[p][:], xd[p][:],
                   stats[:, tc:tc + 1], stats[:, NT + tc:NT + tc + 1],
                   ALU.mult, ALU.add,
                   reads=[("dx", p), ("st1", tc), ("st2", tc)],
                   writes=[("dx", p)])
            ctx.op("vector", "tensor_tensor", xd[p][:], xd[p][:], ad[p][:],
                   ALU.add, reads=[("dx", p), ("da", p)], writes=[("dx", p)])
            ctx.op("vector", "tensor_tensor", xd[p][:], xd[p][:], facc,
                   ALU.add, reads=[("dx", p)] + frd, writes=[("dx", p)])
            ctx.dma("sync", out[tc * 128:(tc + 1) * 128, :], xd[p][:],
                    f"dout{p}", reads=[("dx", p)])
    ctx.finish()
    return nc


def kernel(**inputs):
    cfg = make_cfg(tiny=False)
    nc = build_program(cfg)
    in_maps = [host_core_inputs(cfg, core, inputs) for core in range(8)]
    res = run_bass_kernel_spmd(nc, in_maps, core_ids=list(range(8)))
    B, T, D, CH = cfg["B"], cfg["T"], cfg["D"], cfg["CH"]
    out = np.zeros((B, T, D), dtype=np.float32)
    for core in range(8):
        b, cc = core // 4, core % 4
        o = np.asarray(res.results[core]["out"])
        out[b, cc * CH:(cc + 1) * CH] = o[:CH]
        out[b, (7 - cc) * CH:(8 - cc) * CH] = o[CH:]
    return out
